# version 04
# baseline (speedup 1.0000x reference)
import numpy as np
from contextlib import ExitStack
import concourse.bass as bass
import concourse.mybir as mybir
from concourse.bass_utils import run_bass_kernel_spmd

F32 = mybir.dt.float32
BF16 = mybir.dt.bfloat16
AF = mybir.ActivationFunctionType
ALU = mybir.AluOpType
AX = mybir.AxisListType

T = 2048
TT = 4096
D = 1024
NCH = 8
NSL = T // 512


class Prog:
    def __init__(self, name="prog"):
        self.nc = bass.Bass("TRN2", target_bir_lowering=False, detect_race_conditions=False)
        self.ctx = ExitStack()
        self.sems = {}
        self.cnt = {}
        self.dram = {}
        self.block = None
        self._fin = False

    def __enter__(self):
        self.ctx.__enter__()
        for s in ("ld", "ld2", "mm", "mm2", "vv", "ss", "gp", "st"):
            self.sems[s] = self.ctx.enter_context(self.nc.semaphore(s))
            self.cnt[s] = 0
        self.block = self.ctx.enter_context(self.nc.Block())
        return self

    def __exit__(self, *a):
        return False

    def finish(self):
        if not self._fin:
            self.ctx.__exit__(None, None, None)
            self._fin = True
        return self.nc

    def din(self, name, shape, dtype=F32):
        if name not in self.dram:
            self.dram[name] = self.nc.dram_tensor(name, list(shape), dtype, kind="ExternalInput").ap()
        return self.dram[name]

    def dout(self, name, shape, dtype=F32):
        if name not in self.dram:
            self.dram[name] = self.nc.dram_tensor(name, list(shape), dtype, kind="ExternalOutput").ap()
        return self.dram[name]

    def sb(self, name, shape, dtype=F32):
        return self.ctx.enter_context(self.nc.sbuf_tensor(name, list(shape), dtype))

    def psum(self, name, shape, dtype=F32):
        return self.ctx.enter_context(self.nc.psum_tensor(name, list(shape), dtype))

    def inc(self, inst, sem, n=1):
        inst.then_inc(self.sems[sem], n)
        self.cnt[sem] += n
        return self.cnt[sem]

    def snap(self):
        return dict(self.cnt)

    def wait(self, eng, snap):
        for s, v in snap.items():
            if v > 0:
                eng.wait_ge(self.sems[s], v)


class Common:
    def __init__(self, P):
        self.P = P
        self.identb = P.sb("identb", [128, 128], BF16)
        self.identf = P.sb("identf", [128, 128], F32)
        self.ones_colb = P.sb("ones_colb", [128, 1], BF16)
        self.ones_colf = P.sb("ones_colf", [128, 1], F32)
        self.ones_rowf = P.sb("ones_rowf", [1, 128], F32)
        self.ones_rowb = P.sb("ones_rowb", [1, 128], BF16)
        self.spc = P.sb("cm_spc", [1, 32], F32)

        def body(g):
            for t in (self.identb, self.identf):
                g.memset(t[:], 0.0)
                g.affine_select(out=t[:], in_=t[:], compare_op=ALU.not_equal,
                                fill=1.0, base=0, pattern=[[-1, 128]], channel_multiplier=1)
            g.memset(self.ones_colb[:], 1.0)
            g.memset(self.ones_colf[:], 1.0)
            g.memset(self.ones_rowf[:], 1.0)
            g.memset(self.spc[:], 1.0)
            i = g.memset(self.ones_rowb[:], 1.0)
            P.inc(i, "gp")
        P.block.gpsimd(body)
        self.ready = P.snap()



def snap_max(*snaps):
    out = {}
    for s in snaps:
        for k, v in s.items():
            out[k] = max(out.get(k, 0), v)
    return out

def vgap(cm, vector):
    vector.tensor_copy(cm.spc[:, 0:16], cm.spc[:, 16:32])
    vector.tensor_copy(cm.spc[:, 16:32], cm.spc[:, 0:16])


def mk_ln_scratch(P, tag, tok=T, ps_a=None, ps_b=None):
    if ps_a is not None:
        d = {
            "sq": [P.sb(f"lnsq0_{tag}", [128, 512], F32), P.sb(f"lnsq1_{tag}", [128, 512], F32)],
            "M": P.sb(f"lnM_{tag}", [1, tok], F32),
            "Q": P.sb(f"lnQ_{tag}", [1, tok], F32),
            "W": P.sb(f"lnW_{tag}", [1, tok], F32),
            "tmp": P.sb(f"lntmp_{tag}", [128, 512], F32),
            "ps_s": P.psum(f"lnps_s_{tag}", [1, 512], F32),
            "ps_q": P.psum(f"lnps_q_{tag}", [1, 512], F32),
            "ps_a": ps_a, "ps_b": ps_b,
        }
        return d
    return {
        "sq": [P.sb(f"lnsq0_{tag}", [128, 512], F32), P.sb(f"lnsq1_{tag}", [128, 512], F32)],
        "M": P.sb(f"lnM_{tag}", [1, tok], F32),
        "Q": P.sb(f"lnQ_{tag}", [1, tok], F32),
        "W": P.sb(f"lnW_{tag}", [1, tok], F32),
        "tmp": P.sb(f"lntmp_{tag}", [128, 512], F32),
        "ps_s": P.psum(f"lnps_s_{tag}", [1, 512], F32),
        "ps_q": P.psum(f"lnps_q_{tag}", [1, 512], F32),
        "ps_a": P.psum(f"lnps_a_{tag}", [128, 512], F32),
        "ps_b": P.psum(f"lnps_b_{tag}", [128, 512], F32),
    }


def ln_fm(P, cm, x_fm, out_fm, scr, eps=1e-5, gamma=None, beta=None, pre=None,
          nch=NCH, tok=T):
    nsl = tok // 512
    inv = 1.0 / (nch * 128)
    sq = scr["sq"]
    M, Q, W = scr["M"], scr["Q"], scr["W"]
    ss0, mm0, vv0 = P.cnt["ss"], P.cnt["mm"], P.cnt["vv"]

    def s_body(scalar):
        if pre:
            P.wait(scalar, pre)
        n = 0
        for s in range(nsl):
            sl = slice(512 * s, 512 * (s + 1))
            for c in range(nch):
                if n >= 2:
                    scalar.wait_ge(P.sems["mm"], mm0 + n - 1)
                i = scalar.activation(sq[n % 2][:, :], x_fm[:, c, sl], AF.Square)
                P.inc(i, "ss")
                n += 1
    P.block.scalar(s_body)

    def t_body(tensor):
        if pre:
            P.wait(tensor, pre)
        tensor.wait_ge(P.sems["gp"], 1)
        n = 0
        for s in range(nsl):
            sl = slice(512 * s, 512 * (s + 1))
            if s >= 1:
                tensor.wait_ge(P.sems["vv"], vv0 + s)
            ones_l = cm.ones_colb if x_fm.dtype == BF16 else cm.ones_colf
            for c in range(nch):
                tensor.matmul(scr["ps_s"][:, :], ones_l[:, 0:1], x_fm[:, c, sl],
                              start=(c == 0), stop=(c == nch - 1))
            for c in range(nch):
                tensor.wait_ge(P.sems["ss"], ss0 + n + 1)
                i = tensor.matmul(scr["ps_q"][:, :], cm.ones_colf[:, 0:1], sq[n % 2][:, :],
                                  start=(c == 0), stop=(c == nch - 1))
                P.inc(i, "mm")
                n += 1
    P.block.tensor(t_body)

    def v_body(vector):
        for s in range(nsl):
            sl = slice(512 * s, 512 * (s + 1))
            vector.wait_ge(P.sems["mm"], mm0 + nch * (s + 1))
            vector.tensor_scalar(M[:, sl], scr["ps_s"][:, :], inv, None, ALU.mult)
            i = vector.tensor_scalar(Q[:, sl], scr["ps_q"][:, :], inv, None, ALU.mult)
            P.inc(i, "vv")
    P.block.vector(v_body)
    vv_rows = P.cnt["vv"]

    def v2_body(vector):
        vector.wait_ge(P.sems["vv"], vv_rows)
        vector.tensor_tensor(W[:, :], M[:, :], M[:, :], ALU.mult)
        vector.tensor_tensor(Q[:, :], Q[:, :], W[:, :], ALU.subtract)
        vector.tensor_scalar(Q[:, :], Q[:, :], eps, None, ALU.add)
        i = vector.reciprocal(Q[:, :], Q[:, :])
        P.inc(i, "vv")
    P.block.vector(v2_body)
    vv_rec = P.cnt["vv"]

    def s2_body(scalar):
        scalar.wait_ge(P.sems["vv"], vv_rec)
        i = scalar.activation(W[:, :], Q[:, :], AF.Sqrt)
        P.inc(i, "ss")
    P.block.scalar(s2_body)
    ss_rstd = P.cnt["ss"]

    def v3_body(vector):
        vector.wait_ge(P.sems["ss"], ss_rstd)
        i = vector.tensor_tensor(M[:, :], M[:, :], W[:, :], ALU.mult)
        P.inc(i, "vv")
    P.block.vector(v3_body)
    vv_b = P.cnt["vv"]

    mm1 = P.cnt["mm"]
    vv1 = P.cnt["vv"]

    def t2_body(tensor):
        tensor.wait_ge(P.sems["vv"], vv_b)
        for s in range(nsl):
            if s >= 1:
                tensor.wait_ge(P.sems["vv"], vv1 + s)
            sl = slice(512 * s, 512 * (s + 1))
            tensor.matmul(scr["ps_a"][:, :], cm.ones_rowf[:, :], W[:, sl], start=True, stop=True)
            i = tensor.matmul(scr["ps_b"][:, :], cm.ones_rowf[:, :], M[:, sl], start=True, stop=True)
            P.inc(i, "mm")
    P.block.tensor(t2_body)

    def v4_body(vector):
        for s in range(nsl):
            sl = slice(512 * s, 512 * (s + 1))
            vector.wait_ge(P.sems["mm"], mm1 + s + 1)
            for c in range(nch):
                vector.tensor_tensor(scr["tmp"][:, :], x_fm[:, c, sl], scr["ps_a"][:, :], ALU.mult)
                if gamma is None:
                    i = vector.tensor_tensor(out_fm[:, c, sl], scr["tmp"][:, :], scr["ps_b"][:, :],
                                             ALU.subtract)
                else:
                    vector.tensor_tensor(scr["tmp"][:, :], scr["tmp"][:, :], scr["ps_b"][:, :],
                                         ALU.subtract)
                    i = vector.tensor_scalar(out_fm[:, c, sl], scr["tmp"][:, :],
                                             gamma[:, c:c + 1], beta[:, c:c + 1], ALU.mult, ALU.add)
            P.inc(i, "vv")
    P.block.vector(v4_body)
    return P.snap()


def load_fm(P, dram_ap, sbuf_tile, sem="ld", eng="sync", pre=None):
    def body(e):
        if pre:
            P.wait(e, pre)
        i = e.dma_start(out=sbuf_tile[:, :, :],
                        in_=dram_ap.rearrange("(c p) t -> p c t", p=128))
        P.inc(i, sem, 16)
    getattr(P.block, eng)(body)
    return P.snap()


def store_fm(P, sbuf_tile, dram_ap, wait, sem="st", eng="sync"):
    def body(e):
        P.wait(e, wait)
        i = e.dma_start(out=dram_ap.rearrange("(c p) t -> p c t", p=128), in_=sbuf_tile[:, :, :])
        P.inc(i, sem, 16)
    getattr(P.block, eng)(body)
    return P.snap()


def conv_bf16(P, src, dst, pre=None, eng="gpsimd", sem="gp"):
    def body(e):
        if pre:
            P.wait(e, pre)
        i = e.tensor_copy(dst[:], src[:])
        P.inc(i, sem)
    getattr(P.block, eng)(body)
    return P.snap()


def load_weight_bf16(P, w_dram, wbuf, wbuf16, bias_dram=None, bias_t=None, pre=None):
    def ld_body(sync):
        if pre:
            P.wait(sync, pre)
        i = sync.dma_start(out=wbuf[:, :, :], in_=w_dram.rearrange("(c p) o -> p c o", p=128))
        P.inc(i, "ld", 16)
        if bias_dram is not None:
            i = sync.dma_start(out=bias_t[:, :], in_=bias_dram[:, :])
            P.inc(i, "ld", 16)
    P.block.sync(ld_body)
    return conv_bf16(P, wbuf, wbuf16, pre=P.snap())


def proj_fm(P, cm, wbuf16, x_bf, out_fm, din, dout, tag, bias_t=None, act=AF.Identity,
            scale_col=None, tok=T, pre=None, ps_pair=None, extra_evac=None):
    nic, noc = din // 128, dout // 128
    nsl = tok // 512
    ps = ps_pair
    idxs = [(oc, s) for oc in range(noc) for s in range(nsl)]
    mm0, ss0 = P.cnt["mm"], P.cnt["ss"]

    def t_body(tensor):
        if pre:
            P.wait(tensor, pre)
        for n, (oc, s) in enumerate(idxs):
            sl = slice(512 * s, 512 * (s + 1))
            if n >= 2:
                tensor.wait_ge(P.sems["ss"], ss0 + n - 1)
            pst = ps[n % 2]
            for ic in range(nic):
                i = tensor.matmul(pst[:, :], wbuf16[:, ic, 128 * oc:128 * (oc + 1)],
                                  x_bf[:, ic, sl], start=(ic == 0), stop=(ic == nic - 1))
            P.inc(i, "mm")
    P.block.tensor(t_body)

    def s_body(scalar):
        for n, (oc, s) in enumerate(idxs):
            sl = slice(512 * s, 512 * (s + 1))
            scalar.wait_ge(P.sems["mm"], mm0 + n + 1)
            kw = {}
            if bias_t is not None:
                kw["bias"] = bias_t[:, oc:oc + 1]
            if scale_col is not None:
                kw["scale"] = scale_col[:, oc:oc + 1]
            i = scalar.activation(out_fm[:, oc, sl], ps[n % 2][:, :], act, **kw)
            P.inc(i, "ss")
    P.block.scalar(s_body)
    return P.snap()


def softplus_chain(P, z_view, out_view, scratch2, neg=False, relu_scr=None):
    t1, t2 = scratch2

    def s_ops(scalar):
        scalar.activation(t1, z_view, AF.Abs)
        scalar.activation(t1, t1, AF.Exp, scale=-1.0)
        scalar.activation(t1, t1, AF.Ln, bias=1.0)
        return scalar.activation(t2, z_view, AF.Relu)

    def v_ops(vector):
        if neg:
            return vector.scalar_tensor_tensor(out_view, t1, -1.0, t2, ALU.mult, ALU.subtract)
        return vector.tensor_add(out_view, t1, t2)
    return s_ops, v_ops



def emit_ln0(P, cm, xin="x_fm", xn_out="xn0", lng=None, lnb=None, ln1_out=None,
             ln1_f32=False, tag="ln0"):
    x_d = P.din(xin, [D, T])
    xn_d = P.dout(xn_out, [D, T])
    xbuf = P.sb(f"x_{tag}", [128, NCH, T], F32)
    xnbuf = xbuf
    scr = mk_ln_scratch(P, tag)
    ld = load_fm(P, x_d, xbuf)
    done = ln_fm(P, cm, xbuf, xnbuf, scr, eps=1e-5, pre=ld)
    store_fm(P, xnbuf, xn_d, done)
    if ln1_out is not None:
        g_d = P.din(lng, [128, NCH])
        b_d = P.din(lnb, [128, NCH])
        ln1_d = P.dout(ln1_out, [D, T], F32 if ln1_f32 else BF16)
        gcol = P.sb(f"g_{tag}", [128, NCH], F32)
        bcol = P.sb(f"b_{tag}", [128, NCH], F32)

        def ldgb(sync):
            i = sync.dma_start(out=gcol[:, :], in_=g_d[:, :])
            P.inc(i, "ld", 16)
            i = sync.dma_start(out=bcol[:, :], in_=b_d[:, :])
            P.inc(i, "ld", 16)
        P.block.sync(ldgb)
        ld2 = P.snap()
        ln1buf = P.sb(f"ln1_{tag}", [128, NCH, T], F32 if ln1_f32 else BF16)
        done2 = ln_fm(P, cm, xnbuf, ln1buf, scr, eps=1e-5, gamma=gcol, beta=bcol,
                      pre=snap_max(done, ld2))
        store_fm(P, ln1buf, ln1_d, done2)
    return P.snap()


def tm_to_fm(P, cm, src_bf, out_fm, tt, psT, pre_vv=None, pre=None):
    mm0 = P.cnt["mm"]
    vv0 = P.cnt["vv"]

    def t_body(tensor):
        if pre:
            P.wait(tensor, pre)
        for c in range(NCH):
            if c >= 2:
                tensor.wait_ge(P.sems["vv"], vv0 + c - 1)
            i = tensor.transpose(psT[c % 2][:, :], src_bf[:, 128 * c:128 * (c + 1)], cm.identb[:])
            P.inc(i, "mm")
    P.block.tensor(t_body)

    def v_body(vector):
        for c in range(NCH):
            vector.wait_ge(P.sems["mm"], mm0 + c + 1)
            i = vector.tensor_copy(out_fm[:, c, 128 * tt:128 * (tt + 1)], psT[c % 2][:, :])
            P.inc(i, "vv")
    P.block.vector(v_body)
    return P.snap()


def emit_projA(P, cm):
    ln1_d = P.din("ln1", [D, T], BF16)
    cos_d = P.din("rope_cos", [T, 512])
    sin_d = P.din("rope_sin", [T, 512])

    ln1 = P.sb("pa_ln1", [128, NCH, T], BF16)
    ld0 = load_fm(P, ln1_d, ln1)
    cosb = P.sb("pa_cos", [128, 16, 512], F32)
    sinb = P.sb("pa_sin", [128, 16, 512], F32)

    def ldrope(sync):
        i = sync.dma_start(out=cosb[:, :, :], in_=cos_d.rearrange("(n p) j -> p n j", p=128))
        P.inc(i, "ld", 16)
        i = sync.dma_start(out=sinb[:, :, :], in_=sin_d.rearrange("(n p) j -> p n j", p=128))
        P.inc(i, "ld", 16)
    P.block.sync(ldrope)

    wbuf = P.sb("pa_wf", [128, NCH, D], F32)
    wbuf16 = P.sb("pa_wb", [128, NCH, D], BF16)
    ps = [P.psum("pa_ps0", [128, 512], F32), P.psum("pa_ps1", [128, 512], F32)]
    psT = [P.psum("pa_psT0", [128, 128], BF16), P.psum("pa_psT1", [128, 128], BF16)]
    qt = P.sb("pa_qt", [128, D], F32)
    sqt = P.sb("pa_sq", [128, D], F32)
    nrm = P.sb("pa_nrm", [128, 16], F32)
    rt = P.sb("pa_rt", [128, D], F32)
    rtb = P.sb("pa_rtb", [128, D], BF16)
    out_fm = P.sb("pa_out", [128, NCH, T], BF16)

    def do_qk(w_name, out_name):
        wd = P.din(w_name, [D, D])
        out_d = P.dout(out_name, [D, T], BF16)
        cv = load_weight_bf16(P, wd, wbuf, wbuf16, pre=P.snap())
        for tt in range(16):
            def t_body(tensor, tt=tt):
                P.wait(tensor, cv)
                P.wait(tensor, ld0)
                if tt > 0:
                    tensor.wait_ge(P.sems["vv"], P.cnt["vv"])
                for half in range(2):
                    for ic in range(NCH):
                        i = tensor.matmul(ps[half][:, :],
                                          ln1[:, ic, 128 * tt:128 * (tt + 1)],
                                          wbuf16[:, ic, 512 * half:512 * (half + 1)],
                                          start=(ic == 0), stop=(ic == NCH - 1))
                P.inc(i, "mm")
            P.block.tensor(t_body)
            mm_proj = P.cnt["mm"]

            def v_body(vector):
                vector.wait_ge(P.sems["mm"], mm_proj)
                vector.tensor_copy(qt[:, 0:512], ps[0][:, :])
                i = vector.tensor_copy(qt[:, 512:1024], ps[1][:, :])
                P.inc(i, "vv")
            P.block.vector(v_body)
            vv_evac = P.cnt["vv"]

            def s_body(scalar):
                scalar.wait_ge(P.sems["vv"], vv_evac)
                i = scalar.activation(sqt[:, :], qt[:, :], AF.Square)
                P.inc(i, "ss")
            P.block.scalar(s_body)
            ss_sq = P.cnt["ss"]

            def v2_body(vector):
                vector.wait_ge(P.sems["ss"], ss_sq)
                for h in range(16):
                    vector.reduce_sum(nrm[:, h:h + 1], sqt[:, 64 * h:64 * (h + 1)], axis=AX.X)
                i = vector.engine_nop()
                P.inc(i, "vv")
            P.block.vector(v2_body)
            vv_nrm = P.cnt["vv"]

            def s2_body(scalar):
                scalar.wait_ge(P.sems["vv"], vv_nrm)
                i = scalar.activation(nrm[:, :], nrm[:, :], AF.Sqrt)
                P.inc(i, "ss")
            P.block.scalar(s2_body)
            ss_n = P.cnt["ss"]

            def v3_body(vector, tt=tt):
                vector.wait_ge(P.sems["ss"], ss_n)
                vector.tensor_scalar(nrm[:, :], nrm[:, :], 1e-12, None, ALU.max)
                vector.reciprocal(nrm[:, :], nrm[:, :])
                qe, qo = qt[:, 0:1024:2], qt[:, 1:1024:2]
                re, ro = rt[:, 0:1024:2], rt[:, 1:1024:2]
                cs, sn = cosb[:, tt, :], sinb[:, tt, :]
                vector.tensor_tensor(re, qe, cs, ALU.mult)
                vector.tensor_tensor(ro, qo, cs, ALU.mult)
                vector.tensor_tensor(sqt[:, 0:512], qo, sn, ALU.mult)
                vector.tensor_tensor(sqt[:, 512:1024], qe, sn, ALU.mult)
                vector.tensor_tensor(re, re, sqt[:, 0:512], ALU.subtract)
                vector.tensor_tensor(ro, ro, sqt[:, 512:1024], ALU.add)
                for h in range(16):
                    vector.tensor_scalar(rt[:, 64 * h:64 * (h + 1)], rt[:, 64 * h:64 * (h + 1)],
                                         nrm[:, h:h + 1], None, ALU.mult)
                i = vector.tensor_copy(rtb[:, :], rt[:, :])
                P.inc(i, "vv")
            P.block.vector(v3_body)
            rope_done = P.snap()
            tm_to_fm(P, cm, rtb, out_fm, tt, psT, pre=rope_done)
        store_fm(P, out_fm, out_d, P.snap())

    do_qk("l0_wq", "q0")
    do_qk("l0_wk", "k0")

    v_d = P.dout("v0", [T, D], BF16)
    wd = P.din("l0_wv", [D, D])
    cv = load_weight_bf16(P, wd, wbuf, wbuf16, pre=P.snap())
    vtile = [P.sb("pa_vt0", [128, D], BF16), P.sb("pa_vt1", [128, D], BF16)]
    for tt in range(16):
        def t_body(tensor, tt=tt):
            P.wait(tensor, cv)
            if tt > 0:
                tensor.wait_ge(P.sems["vv"], P.cnt["vv"])
            for half in range(2):
                for ic in range(NCH):
                    i = tensor.matmul(ps[half][:, :],
                                      ln1[:, ic, 128 * tt:128 * (tt + 1)],
                                      wbuf16[:, ic, 512 * half:512 * (half + 1)],
                                      start=(ic == 0), stop=(ic == NCH - 1))
            P.inc(i, "mm")
        P.block.tensor(t_body)
        mm_v = P.cnt["mm"]

        def v_body(vector, tt=tt, st_prev=P.cnt["st"]):
            vector.wait_ge(P.sems["mm"], mm_v)
            if tt >= 2:
                vector.wait_ge(P.sems["st"], st_prev - 16)
            vector.tensor_copy(vtile[tt % 2][:, 0:512], ps[0][:, :])
            i = vector.tensor_copy(vtile[tt % 2][:, 512:1024], ps[1][:, :])
            P.inc(i, "vv")
        P.block.vector(v_body)
        vv_v = P.cnt["vv"]

        def st_body(sync, tt=tt):
            sync.wait_ge(P.sems["vv"], vv_v)
            i = sync.dma_start(out=v_d[128 * tt:128 * (tt + 1), :], in_=vtile[tt % 2][:, :])
            P.inc(i, "st", 16)
        P.block.sync(st_body)

    def fin(sync):
        sync.wait_ge(P.sems["st"], P.cnt["st"])
    P.block.sync(fin)
    return P.snap()


def emit_projB(P, cm):
    ln1_d = P.din("ln1", [D, T], BF16)
    ln1 = P.sb("pb_ln1", [128, NCH, T], BF16)
    ld0 = load_fm(P, ln1_d, ln1)

    wbuf = P.sb("pb_wf", [128, NCH, D], F32)
    wbuf16 = P.sb("pb_wb", [128, NCH, D], BF16)
    bias_t = P.sb("pb_bias", [128, NCH], F32)
    ps = [P.psum("pb_ps0", [128, 512], F32), P.psum("pb_ps1", [128, 512], F32)]
    zt = P.sb("pb_z", [128, NCH, T], F32)
    t1 = P.sb("pb_t1", [128, 512], F32)
    t2 = P.sb("pb_t2", [128, 512], F32)
    red = P.sb("pb_red", [128, NCH], F32)

    def do_gp(w_name, b_name, out_name, neg_softplus, red_name, red_op):
        wd = P.din(w_name, [D, D])
        bd = P.din(b_name, [128, NCH])
        out_d = P.dout(out_name, [D, T])
        red_d = P.dout(red_name, [128, NCH])
        cv = load_weight_bf16(P, wd, wbuf, wbuf16, bias_dram=bd, bias_t=bias_t, pre=P.snap())
        proj_fm(P, cm, wbuf16, ln1, zt, D, D, out_name, bias_t=bias_t, act=AF.Identity,
                ps_pair=ps, pre=snap_max(ld0, cv))
        pj = P.snap()
        if neg_softplus:
            for c in range(NCH):
                for s in range(NSL):
                    sl = slice(512 * s, 512 * (s + 1))
                    zt_v = zt[:, c, sl]
                    s_ops, v_ops = softplus_chain(P, zt_v, zt_v, (t1[:, :], t2[:, :]), neg=True)

                    def s_body(scalar, s_ops=s_ops, first=(c == 0 and s == 0)):
                        if first:
                            P.wait(scalar, pj)
                        else:
                            scalar.wait_ge(P.sems["vv"], P.cnt["vv"])
                        i = s_ops(scalar)
                        P.inc(i, "ss")
                    P.block.scalar(s_body)
                    ss_c = P.cnt["ss"]

                    def v_body(vector, v_ops=v_ops):
                        vector.wait_ge(P.sems["ss"], ss_c)
                        i = v_ops(vector)
                        P.inc(i, "vv")
                    P.block.vector(v_body)
            pj = P.snap()
        def vred(vector):
            P.wait(vector, pj)
            for c in range(NCH):
                i = vector.tensor_reduce(red[:, c:c + 1], zt[:, c, :], AX.X, red_op)
            P.inc(i, "vv")
        P.block.vector(vred)
        rd = P.snap()
        store_fm(P, zt, out_d, pj)

        def st_red(sync):
            P.wait(sync, rd)
            i = sync.dma_start(out=red_d[:, :], in_=red[:, :])
            P.inc(i, "st", 16)
        P.block.sync(st_red)

    do_gp("l0_gate_w", "l0_gate_b", "gj", True, "gjs", ALU.add)
    do_gp("l0_p_w", "l0_p_b", "p", False, "pmx", ALU.max)

    cw_d = P.din("l0_clock_w", [D, 16])
    cb_d = P.din("l0_clock_b", [16])
    ck_d = P.dout("clock", [16, T])
    cwb = P.sb("pb_cw", [128, NCH, 16], F32)
    cwb16 = P.sb("pb_cw16", [128, NCH, 16], BF16)
    cbt = P.sb("pb_cb", [16, 1], F32)
    ps_c = P.psum("pb_psc", [16, 512], F32)
    ck = P.sb("pb_ck", [16, T], F32)
    ct1 = P.sb("pb_ct1", [16, 512], F32)
    ct2 = P.sb("pb_ct2", [16, 512], F32)

    def ldc(sync):
        i = sync.dma_start(out=cwb[:, :, :], in_=cw_d.rearrange("(c p) o -> p c o", p=128))
        P.inc(i, "ld", 16)
        i = sync.dma_start(out=cbt[:, :], in_=cb_d[:, None])
        P.inc(i, "ld", 16)
    P.block.sync(ldc)
    conv_bf16(P, cwb, cwb16, pre=P.snap())
    cvc = P.snap()

    for s in range(NSL):
        sl = slice(512 * s, 512 * (s + 1))

        def t_body(tensor, s=s, sl=sl):
            P.wait(tensor, cvc)
            if s > 0:
                tensor.wait_ge(P.sems["vv"], P.cnt["vv"])
            for ic in range(NCH):
                i = tensor.matmul(ps_c[:, :], cwb16[:, ic, :], ln1[:, ic, sl],
                                  start=(ic == 0), stop=(ic == NCH - 1))
            P.inc(i, "mm")
        P.block.tensor(t_body)
        mm_c = P.cnt["mm"]

        def s_body(scalar, sl=sl, vv_prev=P.cnt["vv"]):
            scalar.wait_ge(P.sems["mm"], mm_c)
            scalar.wait_ge(P.sems["vv"], vv_prev)
            scalar.activation(ck[:, sl], ps_c[:, :], AF.Identity, bias=cbt[:, :])
            scalar.activation(ct1[:, :], ck[:, sl], AF.Abs)
            scalar.activation(ct1[:, :], ct1[:, :], AF.Exp, scale=-1.0)
            scalar.activation(ct1[:, :], ct1[:, :], AF.Ln, bias=1.0)
            i = scalar.activation(ct2[:, :], ck[:, sl], AF.Relu)
            P.inc(i, "ss")
        P.block.scalar(s_body)
        ss_c = P.cnt["ss"]

        def v_body(vector, sl=sl):
            vector.wait_ge(P.sems["ss"], ss_c)
            i = vector.tensor_add(ck[:, sl], ct1[:, :], ct2[:, :])
            P.inc(i, "vv")
        P.block.vector(v_body)

    def stc(sync, w=P.snap()):
        P.wait(sync, w)
        i = sync.dma_start(out=ck_d[:, :], in_=ck[:, :])
        P.inc(i, "st", 16)
        sync.wait_ge(P.sems["st"], P.cnt["st"])
    P.block.sync(stc)
    return P.snap()



def _fmT(a):
    return np.ascontiguousarray(np.asarray(a).T)


def _pc(v, p=128):
    v = np.asarray(v)
    return np.ascontiguousarray(v.reshape(-1, p).T)


def rope_tables(h):
    import math
    half = 32
    freqs = np.exp((-np.arange(half, dtype=np.float32)) * np.float32(math.log(10000.0) / half)).astype(np.float32)
    times = (np.arange(T, dtype=np.float32) + np.float32(2048.0 * h)).astype(np.float32)
    theta = (times[:, None] * freqs[None, :]).astype(np.float32)
    cos = np.cos(theta).astype(np.float32)
    sin = np.sin(theta).astype(np.float32)
    return np.tile(cos, (1, 16)), np.tile(sin, (1, 16))


def emit_gate(P, cm):
    gj_d = P.din("gj", [D, T])
    p_d = P.din("p", [D, T])
    ck_d = P.din("clock", [16, T])
    k0_d = P.din("k0", [D, T], BF16)
    gjp_d = P.din("gj_prev", [128, NCH])
    pmo_d = P.din("pmx_other", [128, NCH])
    pmx_d = P.din("pmx", [128, NCH])
    csel_d = P.din("clock_sel", [16, NCH, 128])

    kg_d = P.dout("kg", [D, T], BF16)
    gcp_d = P.dout("gcp", [D, T])
    pexp_d = P.dout("p_exp", [D, T])
    pes_d = P.dout("pexp_sum", [128, NCH])

    ck = P.sb("gt_ck", [16, T], F32)
    csel = P.sb("gt_csel", [16, NCH, 128], F32)
    gjp = P.sb("gt_gjp", [128, NCH], F32)
    pmx = P.sb("gt_pmx", [128, NCH], F32)
    pmx2 = P.sb("gt_pmx2", [128, NCH], F32)
    npm = P.sb("gt_npm", [128, NCH], F32)
    pes = P.sb("gt_pes", [128, NCH], F32)
    onesT = P.sb("gt_ones", [128, T], F32)
    gjc = P.sb("gt_gj", [128, T], F32)
    pc = P.sb("gt_p", [128, T], F32)
    k0c = P.sb("gt_k0", [128, T], BF16)
    clkb = P.sb("gt_clkb", [128, T], F32)
    kgc = P.sb("gt_kg", [128, T], BF16)
    t1 = P.sb("gt_t1", [128, T], F32)
    t2 = P.sb("gt_t2", [128, T], F32)
    t3 = P.sb("gt_t3", [128, T], F32)
    ps_c = P.psum("gt_psc", [128, 512], F32)

    def ld0(sync):
        for dst, src in [(ck, ck_d), (csel, csel_d), (gjp, gjp_d), (pmx, pmo_d), (pmx2, pmx_d)]:
            i = sync.dma_start(out=dst[:], in_=src[:])
            P.inc(i, "ld", 16)
    P.block.sync(ld0)
    ld_sm = P.snap()

    def vpm(vector):
        P.wait(vector, ld_sm)
        vector.tensor_tensor(pmx[:, :], pmx[:, :], pmx2[:, :], ALU.max)
        vgap(cm, vector)
        i = vector.tensor_scalar(npm[:, :], pmx[:, :], -1.0, None, ALU.mult)
        P.inc(i, "vv")
    P.block.vector(vpm)
    pm_done = P.snap()

    def g_ones(g):
        i = g.memset(onesT[:, :], 1.0)
        P.inc(i, "gp")
    P.block.gpsimd(g_ones)
    ones_done = P.snap()

    for c in range(NCH):
        prev = P.snap()

        def ldc(sync, c=c, prev=prev):
            P.wait(sync, prev)
            i = sync.dma_start(out=gjc[:, :], in_=gj_d.rearrange("(c p) t -> p c t", p=128)[:, c, :])
            P.inc(i, "ld", 16)
            i = sync.dma_start(out=pc[:, :], in_=p_d.rearrange("(c p) t -> p c t", p=128)[:, c, :])
            P.inc(i, "ld", 16)
            i = sync.dma_start(out=k0c[:, :], in_=k0_d.rearrange("(c p) t -> p c t", p=128)[:, c, :])
            P.inc(i, "ld", 16)
        P.block.sync(ldc)
        ld_c = P.snap()
        mm0, vv0 = P.cnt["mm"], P.cnt["vv"]

        def t_clk(tensor, c=c):
            P.wait(tensor, ld_sm)
            for s in range(NSL):
                tensor.wait_ge(P.sems["vv"], vv0 + s)
                i = tensor.matmul(ps_c[:, :], csel[:, c, :], ck[:, 512 * s:512 * (s + 1)],
                                  start=True, stop=True)
                P.inc(i, "mm")
        P.block.tensor(t_clk)

        def v_clk(vector):
            for s in range(NSL):
                vector.wait_ge(P.sems["mm"], mm0 + s + 1)
                i = vector.tensor_copy(clkb[:, 512 * s:512 * (s + 1)], ps_c[:, :])
                P.inc(i, "vv")
        P.block.vector(v_clk)
        clk_done = P.snap()

        def s_pe(scalar, c=c):
            P.wait(scalar, ld_c)
            P.wait(scalar, pm_done)
            i = scalar.activation(t1[:, :], pc[:, :], AF.Exp, bias=npm[:, c:c + 1])
            P.inc(i, "ss")
        P.block.scalar(s_pe)
        ss_pe = P.cnt["ss"]

        def v_g(vector, c=c):
            vector.wait_ge(P.sems["ss"], ss_pe)
            P.wait(vector, clk_done)
            P.wait(vector, ones_done)
            vector.tensor_tensor(t1[:, :], t1[:, :], clkb[:, :], ALU.mult)
            vector.tensor_tensor_scan(t2[:, :], onesT[:, :], gjc[:, :],
                                      gjp[:, c:c + 1], ALU.mult, ALU.add)
            i = vector.tensor_scalar(t2[:, :], t2[:, :], -60.0, 50.0, ALU.max, ALU.min)
            P.inc(i, "vv")
        P.block.vector(v_g)
        vv_g = P.cnt["vv"]

        def s_g(scalar):
            scalar.wait_ge(P.sems["vv"], vv_g)
            i = scalar.activation(gjc[:, :], t2[:, :], AF.Exp)
            P.inc(i, "ss")
        P.block.scalar(s_g)
        ss_g = P.cnt["ss"]

        def v_kg(vector, c=c):
            vector.wait_ge(P.sems["ss"], ss_g)
            vector.tensor_scalar(t2[:, :], gjc[:, :], 1e-8, None, ALU.add)
            vector.reciprocal(t2[:, :], t2[:, :])
            vector.tensor_tensor(t3[:, :], k0c[:, :], t1[:, :], ALU.mult)
            vector.reduce_sum(pes[:, c:c + 1], t1[:, :], axis=AX.X)
            i = vector.tensor_tensor(kgc[:, :], t3[:, :], t2[:, :], ALU.mult)
            P.inc(i, "vv")
        P.block.vector(v_kg)
        kg_done = P.snap()

        def stc(sync, c=c, w=kg_done):
            P.wait(sync, w)
            i = sync.dma_start(out=kg_d.rearrange("(c p) t -> p c t", p=128)[:, c, :], in_=kgc[:, :])
            P.inc(i, "st", 16)
            i = sync.dma_start(out=gcp_d.rearrange("(c p) t -> p c t", p=128)[:, c, :], in_=gjc[:, :])
            P.inc(i, "st", 16)
            i = sync.dma_start(out=pexp_d.rearrange("(c p) t -> p c t", p=128)[:, c, :], in_=t1[:, :])
            P.inc(i, "st", 16)
        P.block.sync(stc)

    def st_fin(sync, w=P.snap()):
        P.wait(sync, w)
        i = sync.dma_start(out=pes_d[:, :], in_=pes[:, :])
        P.inc(i, "st", 16)
        sync.wait_ge(P.sems["st"], P.cnt["st"])
    P.block.sync(st_fin)
    return P.snap()


def safe_recip(P, vector, view, spacer):
    vector.reciprocal(view, view)
    vector.tensor_copy(spacer[:, 0:16], spacer[:, 16:32])
    vector.tensor_copy(spacer[:, 16:32], spacer[:, 0:16])


def emit_attn0(P, cm):
    q0_d = P.din("q0", [D, T], BF16)
    gcp_d = P.din("gcp", [D, T])
    pexp_d = P.din("p_exp", [D, T])
    pep_d = P.din("pexp_prev", [128, NCH])
    kg_d = P.din("kg_all", [D, TT], BF16)
    v_d = P.din("v_all", [TT, D], BF16)
    vis_d = P.din("vis", [128, 2])
    wg_d = P.din("wedge", [4, 128, 512])
    av_d = P.dout("av", [D, T], BF16)

    qg = P.sb("at_qg", [128, NCH, T], BF16)
    kga = P.sb("at_kg", [128, NCH, TT], BF16)
    vaug = P.sb("at_vaug", [128, 32, 65], BF16)
    pt = [P.sb("at_pt0", [128, 512], BF16), P.sb("at_pt1", [128, 512], BF16)]
    wedge = P.sb("at_wedge", [128, 4, 512], F32)
    vis = P.sb("at_vis", [128, 2], F32)
    pep = P.sb("at_pep", [128, NCH], F32)
    onesT = P.sb("at_ones", [128, T], F32)
    spacer = P.sb("at_spc", [1, 32], F32)
    avt = P.sb("at_avt", [64, 512], F32)
    avb = P.sb("at_avb", [64, 512], BF16)
    lr = P.sb("at_lr", [1, 512], F32)
    c1 = P.sb("at_c1", [128, T], F32)
    c2 = P.sb("at_c2", [128, T], F32)
    c3 = P.sb("at_c3", [128, T], BF16)
    ps_s = [P.psum("at_ps0", [128, 512], F32), P.psum("at_ps1", [128, 512], F32)]
    ps_av = P.psum("at_psav", [65, 512], F32)
    ps_b = P.psum("at_psb", [64, 512], F32)

    def ld0(sync):
        for dst, src in [(wedge, wg_d.rearrange("a p t -> p a t")), (vis, vis_d),
                         (pep, pep_d), (kga, kg_d.rearrange("(c p) t -> p c t", p=128))]:
            i = sync.dma_start(out=dst[:], in_=src[:])
            P.inc(i, "ld", 16)
    P.block.sync(ld0)
    ld_k = P.snap()

    def g_init(g):
        g.memset(onesT[:, :], 1.0)
        g.memset(spacer[:, :], 1.0)
        i = g.memset(vaug[:, :, 64:65], 1.0)
        P.inc(i, "gp")
    P.block.gpsimd(g_init)
    init_done = P.snap()

    for c in range(NCH):
        prev = P.snap()

        def ldc(sync, c=c, prev=prev):
            P.wait(sync, prev)
            i = sync.dma_start(out=c1[:, :], in_=gcp_d.rearrange("(c p) t -> p c t", p=128)[:, c, :])
            P.inc(i, "ld", 16)
            i = sync.dma_start(out=c2[:, :], in_=pexp_d.rearrange("(c p) t -> p c t", p=128)[:, c, :])
            P.inc(i, "ld", 16)
            i = sync.dma_start(out=c3[:, :], in_=q0_d.rearrange("(c p) t -> p c t", p=128)[:, c, :])
            P.inc(i, "ld", 16)
        P.block.sync(ldc)
        ld_c = P.snap()

        def v_qg(vector, c=c, ld_c=ld_c):
            P.wait(vector, ld_c)
            P.wait(vector, init_done)
            vector.tensor_tensor_scan(c2[:, :], onesT[:, :], c2[:, :],
                                      pep[:, c:c + 1], ALU.mult, ALU.add)
            vector.tensor_tensor(c1[:, :], c3[:, :], c1[:, :], ALU.mult)
            vector.tensor_scalar(c2[:, :], c2[:, :], 1e-8, None, ALU.add)
            safe_recip(P, vector, c2[:, :], spacer)
            i = vector.tensor_tensor(qg[:, c, :], c1[:, :], c2[:, :], ALU.mult)
            P.inc(i, "vv")
        P.block.vector(v_qg)
    qg_done = P.snap()

    for h in range(16):
        ch, hb = h // 2, 64 * (h % 2)
        prev_h = P.snap()

        def ld_v(sync, h=h, prev_h=prev_h):
            P.wait(sync, prev_h)
            i = sync.dma_start(out=vaug[:, :, 0:64],
                               in_=v_d.rearrange("(k p) d -> p k d", p=128)[:, :, 64 * h:64 * h + 64])
            P.inc(i, "ld", 16)
        P.block.sync(ld_v)
        ld_vh = P.snap()

        for j in range(NSL):
            sl = slice(512 * j, 512 * (j + 1))
            kts = []
            for kt in range(4 * j + 4):
                kts.append((kt, "diag" if kt >= 4 * j else "own"))
            for kt in range(16, 32):
                kts.append((kt, "part"))
            N = len(kts)
            mm0, ss0, vv0, mm20 = P.cnt["mm"], P.cnt["ss"], P.cnt["vv"], P.cnt["mm2"]
            ndiag_before = []
            nd = 0
            for kt, cls in kts:
                ndiag_before.append(nd)
                if cls == "diag":
                    nd += 1

            def t_body(tensor, h=h, ch=ch, hb=hb, sl=sl, kts=kts, N=N,
                       mm0=mm0, ss0=ss0, mm20=mm20, vve=P.cnt["vv"]):
                P.wait(tensor, ld_k)
                P.wait(tensor, ld_vh)
                P.wait(tensor, qg_done)
                for n, (kt, cls) in enumerate(kts):
                    if n >= 2:
                        tensor.wait_ge(P.sems["ss"], ss0 + n - 1)
                    i = tensor.matmul(ps_s[n % 2][:, :],
                                      kga[hb:hb + 64, ch, 128 * kt:128 * (kt + 1)],
                                      qg[hb:hb + 64, ch, sl], start=True, stop=True)
                    P.inc(i, "mm")
                    if n >= 1:
                        kt_p = kts[n - 1][0]
                        tensor.wait_ge(P.sems["ss"], ss0 + n)
                        if n == 1:
                            tensor.wait_ge(P.sems["vv"], vve)
                        i = tensor.matmul(ps_av[:, :], vaug[:, kt_p, :], pt[(n - 1) % 2][:, :],
                                          start=(n - 1 == 0), stop=False)
                        P.inc(i, "mm2")
                tensor.wait_ge(P.sems["ss"], ss0 + N)
                i = tensor.matmul(ps_av[:, :], vaug[:, kts[N - 1][0], :], pt[(N - 1) % 2][:, :],
                                  start=False, stop=True)
                P.inc(i, "mm2")
            P.block.tensor(t_body)

            def v_wedge(vector, kts=kts, mm0=mm0, j=j):
                P.wait(vector, ld_k)
                for n, (kt, cls) in enumerate(kts):
                    if cls != "diag":
                        continue
                    vector.wait_ge(P.sems["mm"], mm0 + n + 1)
                    wi = kt - 4 * j
                    i = vector.tensor_tensor(ps_s[n % 2][:, :], ps_s[n % 2][:, :],
                                             wedge[:, wi, :], ALU.add)
                    P.inc(i, "vv")
            P.block.vector(v_wedge)

            def s_body(scalar, kts=kts, mm0=mm0, vv0=vv0, mm20=mm20, ndb=ndiag_before):
                for n, (kt, cls) in enumerate(kts):
                    if n >= 2:
                        scalar.wait_ge(P.sems["mm2"], mm20 + n - 1)
                    if cls == "diag":
                        scalar.wait_ge(P.sems["vv"], vv0 + ndb[n] + 1)
                        i = scalar.activation(pt[n % 2][:, :], ps_s[n % 2][:, :], AF.Exp,
                                              scale=0.125)
                    elif cls == "own":
                        scalar.wait_ge(P.sems["mm"], mm0 + n + 1)
                        i = scalar.activation(pt[n % 2][:, :], ps_s[n % 2][:, :], AF.Exp,
                                              scale=0.125)
                    else:
                        scalar.wait_ge(P.sems["mm"], mm0 + n + 1)
                        i = scalar.activation(pt[n % 2][:, :], ps_s[n % 2][:, :], AF.Exp,
                                              scale=vis[:, 0:1], bias=vis[:, 1:2])
                    P.inc(i, "ss")
            P.block.scalar(s_body)
            mm2_av = P.cnt["mm2"]

            def v_evac(vector, mm2_av=mm2_av):
                vector.wait_ge(P.sems["mm2"], mm2_av)
                vector.tensor_copy(avt[:, :], ps_av[0:64, :])
                vector.tensor_copy(lr[:, :], ps_av[64:65, :])
                safe_recip(P, vector, lr[:, :], spacer)
                i = vector.engine_nop()
                P.inc(i, "vv")
            P.block.vector(v_evac)
            vv_ev = P.cnt["vv"]

            def t_bc(tensor, vv_ev=vv_ev):
                tensor.wait_ge(P.sems["vv"], vv_ev)
                i = tensor.matmul(ps_b[:, :], cm.ones_rowf[:, 0:64], lr[:, :],
                                  start=True, stop=True)
                P.inc(i, "mm")
            P.block.tensor(t_bc)
            mm_bc = P.cnt["mm"]

            def v_nrm(vector, mm_bc=mm_bc):
                vector.wait_ge(P.sems["mm"], mm_bc)
                i = vector.tensor_tensor(avb[:, :], avt[:, :], ps_b[:, :], ALU.mult)
                P.inc(i, "vv")
            P.block.vector(v_nrm)
            vv_nrm = P.cnt["vv"]

            def st_av(sync, ch=ch, hb=hb, sl=sl, vv_nrm=vv_nrm):
                sync.wait_ge(P.sems["vv"], vv_nrm)
                i = sync.dma_start(
                    out=av_d.rearrange("(c p) t -> p c t", p=128)[hb:hb + 64, ch, sl],
                    in_=avb[:, :])
                P.inc(i, "st", 16)
            P.block.sync(st_av)

    def fin(sync):
        sync.wait_ge(P.sems["st"], P.cnt["st"])
    P.block.sync(fin)
    return P.snap()


def mm_stream(P, cm, w_colview, nob, x_bf, nic, ps_pair, evac_emit, tag,
              wcb, wcb16, pre=None, tb_slices=None, evac_sem="ss", x_of=None,
              cast_dma=False, wcb2=None, wcb16_2=None):
    if tb_slices is None:
        tb_slices = [slice(512 * s, 512 * (s + 1)) for s in range(NSL)]
    if x_of is None:
        x_of = lambda ic, sl: x_bf[:, ic, sl]
    wcbs = [(wcb, wcb16), (wcb2, wcb16_2)] if wcb2 is not None else [(wcb, wcb16)]
    nbuf = len(wcbs)
    n = 0
    ev0 = P.cnt[evac_sem]
    mm_after_oc = []
    for oc in range(nob):
        wf, w16 = wcbs[oc % nbuf]
        if oc < nbuf:
            gate = {"mm": pre.get("mm", 0) if pre else 0, "vv": pre.get("vv", 0) if pre else 0}
        else:
            gate = {"mm": mm_after_oc[oc - nbuf]}

        if cast_dma:
            def ldwg(g, oc=oc, gate=gate, w16=w16):
                P.wait(g, gate)
                if pre and oc == 0:
                    P.wait(g, pre)
                i = g.dma_start(out=w16[:, :, :], in_=w_colview(oc))
                P.inc(i, "ld2", 16)
            P.block.gpsimd(ldwg)
            cv = P.snap()
        else:
            def ldw(sync, oc=oc, gate=gate, wf=wf):
                P.wait(sync, gate)
                if pre and oc == 0:
                    P.wait(sync, pre)
                i = sync.dma_start(out=wf[:, :, :], in_=w_colview(oc))
                P.inc(i, "ld", 16)
            P.block.sync(ldw)
            cv = conv_bf16(P, wf, w16, pre=P.snap())

        mm0 = P.cnt["mm"]

        def t_body(tensor, oc=oc, cv=cv, n0=n, w16=w16):
            P.wait(tensor, cv)
            if pre:
                P.wait(tensor, pre)
            for k, sb in enumerate(tb_slices):
                nn = n0 + k
                if nn >= 2:
                    tensor.wait_ge(P.sems[evac_sem], ev0 + nn - 1)
                for ic in range(nic):
                    i = tensor.matmul(ps_pair[nn % 2][:, :], w16[:, ic, :],
                                      x_of(ic, sb), start=(ic == 0), stop=(ic == nic - 1))
                P.inc(i, "mm")
        P.block.tensor(t_body)
        mm_after_oc.append(P.cnt["mm"])

        for k, sb in enumerate(tb_slices):
            evac_emit(n, oc, sb, mm0 + k + 1)
            assert P.cnt[evac_sem] == ev0 + n + 1, f"evac_emit must inc {evac_sem} once"
            n += 1
    return P.snap()


def emit_mlp0(P, cm, phases=3):
    av_d = P.din("av", [D, T], BF16)
    xn_d = P.din("xn0", [D, T])
    ow_d = P.din("l0_out_w", [D, D])
    ob_d = P.din("l0_out_b", [128, NCH])
    g2_d = P.din("l0_ln2_g", [128, NCH])
    b2_d = P.din("l0_ln2_b", [128, NCH])
    w1_d = P.din("l0_mlp_w1", [D, 4096])
    b1_d = P.din("l0_mlp_b1", [128, 32])
    w2_d = P.din("l0_mlp_w2", [4096, D])
    bb2_d = P.din("l0_mlp_b2", [128, NCH])
    s0_d = P.dout("s0", [D, T])
    x1i_d = P.dout("x1i", [D, T])

    av = P.sb("d_av", [128, NCH, T], BF16)
    xb = P.sb("d_x", [128, NCH, T], F32)
    x1n = P.sb("d_x1n", [128, NCH, T], BF16)
    htb = av[:, :, :].rearrange("p c (a t) -> p (c a) t", a=4)
    wcb = P.sb("d_wcb", [128, 32, 128], F32)
    wcb16 = P.sb("d_wcb16", [128, 32, 128], BF16)
    wcb16b = P.sb("d_wcb16b", [128, 32, 128], BF16)
    cols = P.sb("d_cols", [128, 32 * 4], F32)
    ev1 = P.sb("d_ev1", [128, 512], F32)
    ev1b = P.sb("d_ev1b", [128, 512], F32)
    ps = [P.psum("d_ps0", [128, 512], F32), P.psum("d_ps1", [128, 512], F32)]
    scr = mk_ln_scratch(P, "d")
    ob = cols[:, 0:NCH]
    g2 = cols[:, 8:8 + NCH]
    b2 = cols[:, 16:16 + NCH]
    bb2 = cols[:, 24:24 + NCH]
    b1t = cols[:, 32:64]

    def ld0(sync):
        for dst, src in [(av, av_d.rearrange("(c p) t -> p c t", p=128)),
                         (xb, xn_d.rearrange("(c p) t -> p c t", p=128))]:
            i = sync.dma_start(out=dst[:], in_=src[:])
            P.inc(i, "ld", 16)
        for dst, src in [(ob, ow_d if False else ob_d), (g2, g2_d), (b2, b2_d),
                         (bb2, bb2_d), (b1t, b1_d)]:
            i = sync.dma_start(out=dst, in_=src[:])
            P.inc(i, "ld", 16)
    P.block.sync(ld0)
    ld_done = P.snap()

    def evac1(n, oc, sb, mmn):
        vvp = P.cnt["vv"]

        def s_e(scalar, n=n, oc=oc, sb=sb, mmn=mmn, vvp=vvp):
            scalar.wait_ge(P.sems["mm"], mmn)
            scalar.wait_ge(P.sems["vv"], vvp)
            i = scalar.activation(ev1[:, :], ps[n % 2][:, :], AF.Identity, bias=ob[:, oc:oc + 1])
            P.inc(i, "ss")
        P.block.scalar(s_e)
        ssn = P.cnt["ss"]

        def v_e(vector, n=n, oc=oc, sb=sb, ssn=ssn):
            vector.wait_ge(P.sems["ss"], ssn)
            i = vector.tensor_tensor(xb[:, oc, sb], xb[:, oc, sb], ev1[:, :], ALU.add)
            P.inc(i, "vv")
        P.block.vector(v_e)

    def wview1(oc):
        return ow_d.rearrange("(c p) o -> p c o", p=128)[:, :, 128 * oc:128 * (oc + 1)]

    wcb1v = wcb[:, 0:NCH, :]
    wcb1v16 = wcb16[:, 0:NCH, :]
    mm_stream(P, cm, wview1, NCH, av, NCH, ps, evac1, "d1", wcb1v, wcb1v16,
              pre=ld_done, evac_sem="ss", cast_dma=True,
              wcb2=wcb1v, wcb16_2=wcb16b[:, 0:NCH, :])
    d1 = P.snap()
    store_fm(P, xb, x1i_d, d1)
    if phases < 2:
        def fin0(sync):
            sync.wait_ge(P.sems["st"], P.cnt["st"])
        P.block.sync(fin0)
        return P.snap()

    d2 = ln_fm(P, cm, xb, x1n, scr, eps=1e-5, gamma=g2, beta=b2, pre=d1)
    if phases < 3:
        def fin1(sync):
            sync.wait_ge(P.sems["st"], P.cnt["st"])
        P.block.sync(fin1)
        return P.snap()

    for tb in range(NSL):
        sb0 = slice(512 * tb, 512 * (tb + 1))

        def evac_h(n, oc, sb, mmn, tb=tb):

            def s_e(scalar, n=n, oc=oc, mmn=mmn):
                scalar.wait_ge(P.sems["mm"], mmn)
                i = scalar.activation(htb[:, oc, :], ps[n % 2][:, :], AF.Gelu,
                                      bias=b1t[:, oc:oc + 1])
                P.inc(i, "ss")
            P.block.scalar(s_e)

        def wview_h(oc):
            return w1_d.rearrange("(c p) o -> p c o", p=128)[:, :, 128 * oc:128 * (oc + 1)]

        mm_stream(P, cm, wview_h, 32, x1n, NCH, ps, evac_h, f"dh{tb}",
                  wcb[:, 0:NCH, :], wcb16[:, 0:NCH, :], pre=P.snap(),
                  tb_slices=[sb0], evac_sem="ss", cast_dma=True,
                  wcb2=wcb[:, 0:NCH, :], wcb16_2=wcb16b[:, 0:NCH, :])
        h_done = P.snap()

        def evac_o(n, oc, sb, mmn, tb=tb):
            vvp = P.cnt["vv"]

            def s_e(scalar, n=n, oc=oc, mmn=mmn, vvp=vvp):
                scalar.wait_ge(P.sems["mm"], mmn)
                scalar.wait_ge(P.sems["vv"], vvp)
                i = scalar.activation(ev1[:, :], ps[n % 2][:, :], AF.Identity,
                                      bias=bb2[:, oc:oc + 1])
                P.inc(i, "ss")
            P.block.scalar(s_e)
            ssn = P.cnt["ss"]
            stp = P.cnt["st"]

            gsb = slice(512 * tb, 512 * (tb + 1))

            def v_e(vector, n=n, oc=oc, gsb=gsb, ssn=ssn, stp=stp):
                vector.wait_ge(P.sems["ss"], ssn)
                vector.wait_ge(P.sems["st"], stp)
                i = vector.tensor_tensor(ev1b[:, :], xb[:, oc, gsb], ev1[:, :], ALU.add)
                P.inc(i, "vv")
            P.block.vector(v_e)
            vvn = P.cnt["vv"]

            def st_e(sync, oc=oc, gsb=gsb, vvn=vvn):
                sync.wait_ge(P.sems["vv"], vvn)
                i = sync.dma_start(out=s0_d.rearrange("(c p) t -> p c t", p=128)[:, oc, gsb],
                                   in_=ev1b[:, :])
                P.inc(i, "st", 16)
            P.block.sync(st_e)

        def wview_o(oc):
            return w2_d.rearrange("(c p) o -> p c o", p=128)[:, :, 128 * oc:128 * (oc + 1)]

        mm_stream(P, cm, wview_o, NCH, htb, 32, ps, evac_o, f"do{tb}",
                  wcb, wcb16, pre=h_done, tb_slices=[slice(0, 512)], evac_sem="vv",
                  x_of=lambda ic, sl: htb[:, ic, :], cast_dma=True,
                  wcb2=wcb, wcb16_2=wcb16b)

    def fin(sync):
        sync.wait_ge(P.sems["st"], P.cnt["st"])
    P.block.sync(fin)
    return P.snap()


def emit_moe(P, cm, dbg=False):
    xn_d = P.din("xn0", [D, T])
    wm_d = P.din("moe_wm", [D, D])
    bm_d = P.din("moe_bm", [128, NCH])
    emb_d = P.din("moe_emb", [128, 16])
    w1_d = P.din("moe_w1", [16, 128, 512])
    b1_d = P.din("moe_b1", [16, 128, 4])
    w2_d = P.din("moe_w2", [16, 512, 128])
    b2_d = P.din("moe_b2", [16, 128, 1])
    wo_d = P.din("moe_wo", [D, D])
    bo_d = P.din("moe_bo", [128, NCH])
    x_d = P.din("x_fm", [D, T])
    s0_d = P.din("s0", [D, T])
    x1_d = P.dout("x1", [D, T])
    wsT_d = P.nc.dram_tensor("wsT", [NCH, 16, T], F32,
                             kind="ExternalOutput" if dbg else "Internal").ap()
    if dbg:
        hb_d = P.dout("hb_dbg", [D, T], BF16)
        comb_d = P.dout("comb_dbg", [D, T], BF16)

    xn = P.sb("e_xn", [128, NCH, T], BF16)
    hb = P.sb("e_hb", [128, NCH, T], BF16)
    comb = P.sb("e_comb", [128, NCH, T], BF16)
    emb = P.sb("e_emb", [128, 16], BF16)
    embf = P.sb("e_embf", [128, 16], F32)
    wcb = P.sb("e_wcb", [128, NCH, 128], F32)
    wcb16 = P.sb("e_wcb16", [128, NCH, 128], BF16)
    wcb16b = P.sb("e_wcb16b", [128, NCH, 128], BF16)
    w1e = P.sb("e_w1e", [128, 512], F32)
    w1e16 = P.sb("e_w1e16", [128, 512], BF16)
    w2e = P.sb("e_w2e", [128, 4, 128], F32)
    w2e16 = P.sb("e_w2e16", [128, 4, 128], BF16)
    bcols = P.sb("e_bcols", [128, 32], F32)
    hm = P.sb("e_hm", [128, 4, 512], BF16)
    wsl = P.sb("e_wsl", [1, T], F32)
    wsT = P.sb("e_wsT", [16, T], F32)
    z = P.sb("e_z", [128, 16], F32)
    zw = P.sb("e_zw", [128, 16], F32)
    zcol = P.sb("e_zcol", [128, 4], F32)
    eot = P.sb("e_eot", [128, 512], F32)
    prod = P.sb("e_prod", [128, 512], F32)
    s0t = P.sb("e_s0t", [128, 512], F32)
    xt = P.sb("e_xt", [128, 512], F32)
    ps = [P.psum("e_ps0", [128, 512], F32), P.psum("e_ps1", [128, 512], F32)]
    ps_l = P.psum("e_psl", [128, 16], F32)
    ps_w = P.psum("e_psw", [16, 128], F32)
    ps_b = P.psum("e_psb", [128, 512], F32)

    bm = bcols[:, 0:8]
    bo = bcols[:, 8:16]
    b1e = bcols[:, 16:20]
    b2e = bcols[:, 20:21]

    def ld0(sync):
        for dst, src in [(embf[:, :], emb_d), (bm, bm_d), (bo, bo_d)]:
            i = sync.dma_start(out=dst, in_=src[:])
            P.inc(i, "ld", 16)
    P.block.sync(ld0)

    def ld0g(g):
        i = g.dma_start(out=xn[:, :, :], in_=xn_d.rearrange("(c p) t -> p c t", p=128))
        P.inc(i, "ld2", 16)
    P.block.gpsimd(ld0g)
    conv_bf16(P, embf, emb, pre=P.snap())
    ld_done = P.snap()

    def evac_h(n, oc, sb, mmn):
        def s_e(scalar, n=n, oc=oc, sb=sb, mmn=mmn):
            scalar.wait_ge(P.sems["mm"], mmn)
            i = scalar.activation(hb[:, oc, sb], ps[n % 2][:, :], AF.Identity,
                                  bias=bm[:, oc:oc + 1])
            P.inc(i, "ss")
        P.block.scalar(s_e)

    def wview_m(oc):
        return wm_d.rearrange("(c p) o -> p c o", p=128)[:, :, 128 * oc:128 * (oc + 1)]

    mm_stream(P, cm, wview_m, NCH, xn, NCH, ps, evac_h, "e1", wcb, wcb16,
              pre=ld_done, evac_sem="ss", cast_dma=True, wcb2=wcb, wcb16_2=wcb16b)
    e1 = P.snap()

    for c in range(NCH):
        for tt in range(16):
            mm0 = P.cnt["mm"]
            vvp = P.cnt["vv"]

            def t_l(tensor, c=c, tt=tt, vvp=vvp):
                P.wait(tensor, e1)
                tensor.wait_ge(P.sems["vv"], vvp)
                i = tensor.matmul(ps_l[:, :], hb[:, c, 128 * tt:128 * (tt + 1)], emb[:, :],
                                  start=True, stop=True)
                P.inc(i, "mm")
            P.block.tensor(t_l)
            mm_l = P.cnt["mm"]

            def v_sm(vector, mm_l=mm_l):
                vector.wait_ge(P.sems["mm"], mm_l)
                vector.tensor_reduce(zcol[:, 0:1], ps_l[:, :], AX.X, ALU.max, negate=True)
                i = vector.tensor_copy(z[:, :], ps_l[:, :])
                P.inc(i, "vv")
            P.block.vector(v_sm)
            vv_sm = P.cnt["vv"]

            def s_sm(scalar, vv_sm=vv_sm):
                scalar.wait_ge(P.sems["vv"], vv_sm)
                i = scalar.activation(z[:, :], z[:, :], AF.Exp, bias=zcol[:, 0:1])
                P.inc(i, "ss")
            P.block.scalar(s_sm)
            ss_sm = P.cnt["ss"]

            def v_t2(vector, ss_sm=ss_sm):
                vector.wait_ge(P.sems["ss"], ss_sm)
                vector.reduce_sum(zcol[:, 1:2], z[:, :], axis=AX.X)
                vgap(cm, vector)
                safe_recip(P, vector, zcol[:, 1:2], cm.spc)
                vector.tensor_scalar(z[:, :], z[:, :], zcol[:, 1:2], None, ALU.mult)
                vgap(cm, vector)
                vector.reduce_max(zcol[:, 2:3], z[:, :], axis=AX.X)
                vgap(cm, vector)
                vector.tensor_scalar(zw[:, :], z[:, :], zcol[:, 2:3], None, ALU.is_lt)
                vgap(cm, vector)
                vector.tensor_tensor(zw[:, :], z[:, :], zw[:, :], ALU.mult)
                vgap(cm, vector)
                vector.reduce_max(zcol[:, 3:4], zw[:, :], axis=AX.X)
                vgap(cm, vector)
                vector.tensor_scalar(zw[:, :], z[:, :], zcol[:, 3:4], None, ALU.is_ge)
                vgap(cm, vector)
                i = vector.tensor_tensor(zw[:, :], z[:, :], zw[:, :], ALU.mult)
                P.inc(i, "vv")
            P.block.vector(v_t2)
            vv_t2 = P.cnt["vv"]

            def t_tr(tensor, vv_t2=vv_t2):
                tensor.wait_ge(P.sems["vv"], vv_t2)
                i = tensor.transpose(ps_w[:, :], zw[:, :], cm.identf[:])
                P.inc(i, "mm")
            P.block.tensor(t_tr)
            mm_tr = P.cnt["mm"]

            def v_tr(vector, tt=tt, mm_tr=mm_tr):
                vector.wait_ge(P.sems["mm"], mm_tr)
                i = vector.tensor_copy(wsT[:, 128 * tt:128 * (tt + 1)], ps_w[:, :])
                P.inc(i, "vv")
            P.block.vector(v_tr)

        def st_w(sync, c=c, w=P.snap()):
            P.wait(sync, w)
            i = sync.dma_start(out=wsT_d[c, :, :], in_=wsT[:, :])
            P.inc(i, "st", 16)
        P.block.sync(st_w)
    e2 = P.snap()

    def g_zero(g):
        P.wait(g, e1)
        i = g.memset(comb[:, :, :], 0.0)
        P.inc(i, "gp")
    P.block.gpsimd(g_zero)
    zero_done = P.snap()

    w1b2 = [w1e16, P.sb("e_w1e16b", [128, 512], BF16)]
    w2b2 = [w2e16, P.sb("e_w2e16b", [128, 4, 128], BF16)]
    wsl2 = [wsl, P.sb("e_wslb", [1, T], F32)]
    b12 = [bcols[:, 16:20], bcols[:, 22:26]]
    b22 = [bcols[:, 20:21], bcols[:, 26:27]]
    mmE = []
    k = 0
    for c in range(NCH):
        for e in range(16):
            bi = k % 2
            gate = ({"mm": mmE[k - 2][0], "ss": mmE[k - 2][1]} if k >= 2 else {})

            def ld_e(g, c=c, e=e, k=k, bi=bi, gate=gate):
                P.wait(g, gate)
                if k == 0:
                    P.wait(g, e2)
                i = g.dma_start(out=wsl2[bi][:, :], in_=wsT_d[c, e:e + 1, :])
                P.inc(i, "ld2", 16)
                i = g.dma_start(out=w1b2[bi][:, :], in_=w1_d[e, :, :])
                P.inc(i, "ld2", 16)
                i = g.dma_start(out=w2b2[bi][:, :, :], in_=w2_d[e, :, :].rearrange("(a p) o -> p a o", p=128))
                P.inc(i, "ld2", 16)
                i = g.dma_start(out=b12[bi], in_=b1_d[e, :, :])
                P.inc(i, "ld2", 16)
                i = g.dma_start(out=b22[bi], in_=b2_d[e, :, :])
                P.inc(i, "ld2", 16)
            P.block.gpsimd(ld_e)
            ld_e2 = {"ld2": P.cnt["ld2"]}

            for s in range(NSL):
                sb = slice(512 * s, 512 * (s + 1))
                mm0 = P.cnt["mm"]
                ssp = P.cnt["ss"]

                def t_hm(tensor, c=c, sb=sb, bi=bi, ld_e2=ld_e2, ssp=ssp):
                    P.wait(tensor, ld_e2)
                    tensor.wait_ge(P.sems["ss"], ssp)
                    for ic in range(4):
                        if ic >= 2:
                            tensor.wait_ge(P.sems["ss"], ssp + ic - 1)
                        i = tensor.matmul(ps[ic % 2][:, :], w1b2[bi][:, 128 * ic:128 * (ic + 1)],
                                          hb[:, c, sb], start=True, stop=True)
                        P.inc(i, "mm")
                P.block.tensor(t_hm)

                def s_hm(scalar, mm0=mm0, bi=bi):
                    for ic in range(4):
                        scalar.wait_ge(P.sems["mm"], mm0 + ic + 1)
                        i = scalar.activation(hm[:, ic, :], ps[ic % 2][:, :], AF.Gelu,
                                              bias=b12[bi][:, ic:ic + 1])
                        P.inc(i, "ss")
                P.block.scalar(s_hm)
                hm_done = P.cnt["ss"]
                mm1 = P.cnt["mm"]
                vvp2 = P.cnt["vv"]

                def t_eo(tensor, sb=sb, bi=bi, hm_done=hm_done, vvp2=vvp2):
                    tensor.wait_ge(P.sems["ss"], hm_done)
                    tensor.wait_ge(P.sems["vv"], vvp2)
                    for ic in range(4):
                        tensor.matmul(ps[0][:, :], w2b2[bi][:, ic, :], hm[:, ic, :],
                                      start=(ic == 0), stop=(ic == 3))
                    i = tensor.matmul(ps_b[:, :], cm.ones_rowf[:, :], wsl2[bi][:, sb],
                                      start=True, stop=True)
                    P.inc(i, "mm")
                P.block.tensor(t_eo)
                mm_eo = P.cnt["mm"]

                def s_eo(scalar, mm_eo=mm_eo, bi=bi):
                    scalar.wait_ge(P.sems["mm"], mm_eo)
                    i = scalar.activation(eot[:, :], ps[0][:, :], AF.Identity,
                                          bias=b22[bi][:, 0:1])
                    P.inc(i, "ss")
                P.block.scalar(s_eo)
                ss_eo = P.cnt["ss"]

                def v_cb(vector, c=c, sb=sb, ss_eo=ss_eo):
                    vector.wait_ge(P.sems["ss"], ss_eo)
                    P.wait(vector, zero_done)
                    vector.tensor_tensor(prod[:, :], eot[:, :], ps_b[:, :], ALU.mult)
                    i = vector.tensor_tensor(comb[:, c, sb], comb[:, c, sb], prod[:, :], ALU.add)
                    P.inc(i, "vv")
                P.block.vector(v_cb)
            mmE.append((P.cnt["mm"], P.cnt["ss"]))
            k += 1
    e3 = P.snap()

    def evac_o(n, oc, sb, mmn):
        ldp = P.cnt["ld2"]

        def ld_xs(gp_, oc=oc, sb=sb, n=n):
            if n >= 1:
                P.wait(gp_, {"vv": P.cnt["vv"]})
            i = gp_.dma_start(out=s0t[:, :], in_=s0_d.rearrange("(c p) t -> p c t", p=128)[:, oc, sb])
            P.inc(i, "ld2", 16)
            i = gp_.dma_start(out=xt[:, :], in_=x_d.rearrange("(c p) t -> p c t", p=128)[:, oc, sb])
            P.inc(i, "ld2", 16)
        P.block.gpsimd(ld_xs)
        ld_n = P.cnt["ld2"]

        def s_e(scalar, n=n, oc=oc, mmn=mmn):
            scalar.wait_ge(P.sems["mm"], mmn)
            scalar.wait_ge(P.sems["vv"], P.cnt["vv"])
            i = scalar.activation(eot[:, :], ps[n % 2][:, :], AF.Silu, bias=bo[:, oc:oc + 1])
            P.inc(i, "ss")
        P.block.scalar(s_e)
        ssn = P.cnt["ss"]
        stp = P.cnt["st"]

        def v_e(vector, oc=oc, sb=sb, ssn=ssn, stp=stp, ld_n=ld_n):
            vector.wait_ge(P.sems["ss"], ssn)
            vector.wait_ge(P.sems["ld2"], ld_n)
            vector.wait_ge(P.sems["st"], stp)
            vector.tensor_tensor(prod[:, :], eot[:, :], s0t[:, :], ALU.mult)
            i = vector.tensor_tensor(xt[:, :], xt[:, :], prod[:, :], ALU.add)
            P.inc(i, "vv")
        P.block.vector(v_e)
        vvn = P.cnt["vv"]

        def st_e(sync, oc=oc, sb=sb, vvn=vvn):
            sync.wait_ge(P.sems["vv"], vvn)
            i = sync.dma_start(out=x1_d.rearrange("(c p) t -> p c t", p=128)[:, oc, sb],
                               in_=xt[:, :])
            P.inc(i, "st", 16)
        P.block.sync(st_e)

    def wview_o(oc):
        return wo_d.rearrange("(c p) o -> p c o", p=128)[:, :, 128 * oc:128 * (oc + 1)]

    if dbg:
        store_fm(P, hb, hb_d, e3)
        store_fm(P, comb, comb_d, e3)
        e3 = P.snap()
    mm_stream(P, cm, wview_o, NCH, comb, NCH, ps, evac_o, "e4", wcb, wcb16,
              pre=e3, evac_sem="vv", cast_dma=True, wcb2=wcb, wcb16_2=wcb16b)

    def fin(sync):
        sync.wait_ge(P.sems["st"], P.cnt["st"])
    P.block.sync(fin)
    return P.snap()


def emit_lnx(P, cm, xin, tag, out_bf=None, out_tm=None, out_f32=None):
    x_d = P.din(xin, [D, T])
    xbuf = P.sb(f"lx_{tag}", [128, NCH, T], F32)
    xb16 = P.sb(f"lxb_{tag}", [128, NCH, T], BF16)
    scr = mk_ln_scratch(P, tag)
    ld = load_fm(P, x_d, xbuf)
    done = ln_fm(P, cm, xbuf, xbuf, scr, eps=1e-5, pre=ld)
    if out_f32 is not None:
        store_fm(P, xbuf, P.dout(out_f32, [D, T]), done)
    cv = conv_bf16(P, xbuf, xb16, pre=done, eng="vector", sem="vv")
    if out_bf is not None:
        store_fm(P, xb16, P.dout(out_bf, [D, T], BF16), cv)
    if out_tm is not None:
        tm_d = P.dout(out_tm, [T, D], BF16)
        psT = [P.psum(f"lxps0_{tag}", [128, 128], BF16), P.psum(f"lxps1_{tag}", [128, 128], BF16)]
        tmt = [P.sb(f"lxt0_{tag}", [128, D], BF16), P.sb(f"lxt1_{tag}", [128, D], BF16)]
        for tt in range(16):
            mm0 = P.cnt["mm"]
            vv0 = P.cnt["vv"]
            stp = P.cnt["st"]

            def t_b(tensor, tt=tt, cv=cv, mm0=mm0, vv0=vv0):
                P.wait(tensor, cv)
                for c in range(NCH):
                    if c >= 2:
                        tensor.wait_ge(P.sems["vv"], vv0 + c - 1)
                    i = tensor.transpose(psT[c % 2][:, :], xb16[:, c, 128 * tt:128 * (tt + 1)],
                                         cm.identb[:])
                    P.inc(i, "mm")
            P.block.tensor(t_b)

            def v_b(vector, tt=tt, mm0=mm0, stp=stp):
                if tt >= 2:
                    vector.wait_ge(P.sems["st"], stp - 16)
                for c in range(NCH):
                    vector.wait_ge(P.sems["mm"], mm0 + c + 1)
                    i = vector.tensor_copy(tmt[tt % 2][:, 128 * c:128 * (c + 1)], psT[c % 2][:, :])
                    P.inc(i, "vv")
            P.block.vector(v_b)
            vvn = P.cnt["vv"]

            def st_b(sync, tt=tt, vvn=vvn):
                sync.wait_ge(P.sems["vv"], vvn)
                i = sync.dma_start(out=tm_d[128 * tt:128 * (tt + 1), :], in_=tmt[tt % 2][:, :])
                P.inc(i, "st", 16)
            P.block.sync(st_b)

    def fin(sync):
        sync.wait_ge(P.sems["st"], P.cnt["st"])
    P.block.sync(fin)
    return P.snap()


def emit_l1_attn(P, cm):
    xnb_d = P.din("xn1b", [D, T], BF16)
    xfm_d = P.din("xn1_all", [D, TT], BF16)
    xtm_d = P.din("xn1_all_tm", [TT, D], BF16)
    w_d = P.din("l1_w", [D, D])
    b_d = P.din("l1_b", [128, NCH])
    lw_d = P.din("l1_lw", [TT, T])
    s1_d = P.dout("s1", [D, T])

    xnb = P.sb("f_xnb", [128, NCH, T], BF16)
    xtm = P.sb("f_xtm", [128, 32, D], BF16)
    xtr = P.sb("f_xtr", [128, NCH, T], BF16)
    fmkt = [P.sb("f_fm0", [128, NCH, 128], BF16), P.sb("f_fm1", [128, NCH, 128], BF16)]
    lwt = [P.sb("f_lw0", [128, 512], F32), P.sb("f_lw1", [128, 512], F32)]
    wcol = P.sb("f_wcol", [128, 32, 512], BF16)
    zt = P.sb("f_zt", [128, 512], F32)
    mskt = P.sb("f_msk", [128, 512], F32)
    lr = P.sb("f_lr", [1, 512], F32)
    ev = P.sb("f_ev", [128, 512], F32)
    wcb = P.sb("f_wcb", [128, NCH, 128], F32)
    wcb16 = P.sb("f_wcb16", [128, NCH, 128], BF16)
    wcb16b = P.sb("f_wcb16b", [128, NCH, 128], BF16)
    bias_t = P.sb("f_bias", [128, NCH], F32)
    ps = [P.psum("f_ps0", [128, 512], F32), P.psum("f_ps1", [128, 512], F32)]
    ps_l = P.psum("f_psl", [1, 512], F32)
    ps_b = P.psum("f_psb", [128, 512], F32)

    def ld0(sync):
        i = sync.dma_start(out=bias_t[:, :], in_=b_d[:, :])
        P.inc(i, "ld", 16)
        i = sync.dma_start(out=xnb[:, :, :], in_=xnb_d.rearrange("(c p) t -> p c t", p=128))
        P.inc(i, "ld", 16)
        i = sync.dma_start(out=xtm[:, :, :], in_=xtm_d.rearrange("(k p) d -> p k d", p=128))
        P.inc(i, "ld", 16)
    P.block.sync(ld0)
    ld_done = P.snap()

    def evac_tr(n, oc, sb, mmn):
        def s_e(scalar, n=n, oc=oc, sb=sb, mmn=mmn):
            scalar.wait_ge(P.sems["mm"], mmn)
            i = scalar.activation(xtr[:, oc, sb], ps[n % 2][:, :], AF.Identity,
                                  bias=bias_t[:, oc:oc + 1])
            P.inc(i, "ss")
        P.block.scalar(s_e)

    def wview(oc):
        return w_d.rearrange("(c p) o -> p c o", p=128)[:, :, 128 * oc:128 * (oc + 1)]

    mm_stream(P, cm, wview, NCH, xnb, NCH, ps, evac_tr, "f_tr", wcb, wcb16,
              pre=ld_done, evac_sem="ss", cast_dma=True, wcb2=wcb, wcb16_2=wcb16b)
    tr_done = P.snap()

    for qs in range(NSL):
        sb = slice(512 * qs, 512 * (qs + 1))
        mmF = []
        for kt in range(32):
            prev_ld = P.snap()
            gate = ({"mm": mmF[kt - 2][0], "vv": mmF[kt - 2][1]} if kt >= 2
                    else {"mm": prev_ld["mm"], "vv": prev_ld["vv"]} if qs > 0 or kt < 2
                    else {})

            def ld_kt(g, kt=kt, qs=qs, gate=gate, buf=kt % 2):
                P.wait(g, gate)
                i = g.dma_start(out=fmkt[kt % 2][:, :, :],
                                in_=xfm_d.rearrange("(c p) t -> p c t", p=128)[:, :, 128 * kt:128 * (kt + 1)])
                P.inc(i, "ld2", 16)
                i = g.dma_start(out=lwt[kt % 2][:, :],
                                in_=lw_d.rearrange("(k p) t -> p k t", p=128)[:, kt, sb])
                P.inc(i, "ld2", 16)
            P.block.gpsimd(ld_kt)
            ld_kt_n = P.cnt["ld2"]
            vvp = P.cnt["vv"]

            def t_sc(tensor, kt=kt, sb=sb, ld_kt_n=ld_kt_n, vvp=vvp):
                tensor.wait_ge(P.sems["ld2"], ld_kt_n)
                P.wait(tensor, tr_done)
                tensor.wait_ge(P.sems["vv"], vvp)
                for c in range(NCH):
                    i = tensor.matmul(ps[kt % 2][:, :], fmkt[kt % 2][:, c, :], xtr[:, c, sb],
                                      start=(c == 0), stop=(c == NCH - 1))
                P.inc(i, "mm")
            P.block.tensor(t_sc)
            mm_sc = P.cnt["mm"]
            ssp = P.cnt["ss"]

            def v_z(vector, kt=kt, mm_sc=mm_sc, ssp=ssp):
                vector.wait_ge(P.sems["mm"], mm_sc)
                vector.wait_ge(P.sems["ss"], ssp)
                vector.tensor_tensor(zt[:, :], ps[kt % 2][:, :], lwt[kt % 2][:, :], ALU.mult)
                i = vector.tensor_single_scalar(mskt[:, :], lwt[kt % 2][:, :], 0.0, ALU.is_gt)
                P.inc(i, "vv")
            P.block.vector(v_z)
            vv_z = P.cnt["vv"]

            def s_z(scalar, vv_z=vv_z):
                scalar.wait_ge(P.sems["vv"], vv_z)
                scalar.activation(zt[:, :], zt[:, :], AF.Tanh)
                i = scalar.activation(zt[:, :], zt[:, :], AF.Exp)
                P.inc(i, "ss")
            P.block.scalar(s_z)
            ss_z = P.cnt["ss"]

            def v_w(vector, kt=kt, ss_z=ss_z):
                vector.wait_ge(P.sems["ss"], ss_z)
                i = vector.tensor_tensor(wcol[:, kt, :], zt[:, :], mskt[:, :], ALU.mult)
                P.inc(i, "vv")
            P.block.vector(v_w)
            mmF.append((P.cnt["mm"], P.cnt["vv"]))
        w_done = P.snap()

        def t_l(tensor, w_done=w_done):
            P.wait(tensor, w_done)
            for kt in range(32):
                i = tensor.matmul(ps_l[:, :], cm.ones_colb[:, 0:1], wcol[:, kt, :],
                                  start=(kt == 0), stop=(kt == 31))
            P.inc(i, "mm")
        P.block.tensor(t_l)
        mm_l = P.cnt["mm"]

        def v_l(vector, mm_l=mm_l):
            vector.wait_ge(P.sems["mm"], mm_l)
            vector.tensor_scalar(lr[:, :], ps_l[:, :], 1e-7, None, ALU.add)
            safe_recip(P, vector, lr[:, :], cm.spc)
            i = vector.engine_nop()
            P.inc(i, "vv")
        P.block.vector(v_l)
        vv_l = P.cnt["vv"]

        def t_lb(tensor, vv_l=vv_l):
            tensor.wait_ge(P.sems["vv"], vv_l)
            i = tensor.matmul(ps_b[:, :], cm.ones_rowf[:, :], lr[:, :], start=True, stop=True)
            P.inc(i, "mm")
        P.block.tensor(t_lb)
        mm_lb = P.cnt["mm"]

        for dch in range(NCH):
            mm0 = P.cnt["mm"]
            vvp = P.cnt["vv"]

            def t_v(tensor, dch=dch, vvp=vvp):
                tensor.wait_ge(P.sems["vv"], vvp)
                for kt in range(32):
                    i = tensor.matmul(ps[dch % 2][:, :], xtm[:, kt, 128 * dch:128 * (dch + 1)],
                                      wcol[:, kt, :], start=(kt == 0), stop=(kt == 31))
                P.inc(i, "mm")
            P.block.tensor(t_v)
            mm_v = P.cnt["mm"]
            stp = P.cnt["st"]

            def v_v(vector, dch=dch, mm_v=mm_v, stp=stp, mm_lb=mm_lb):
                vector.wait_ge(P.sems["mm"], max(mm_v, mm_lb))
                vector.wait_ge(P.sems["st"], stp)
                vector.tensor_copy(ev[:, :], ps[dch % 2][:, :])
                i = vector.tensor_tensor(ev[:, :], ev[:, :], ps_b[:, :], ALU.mult)
                P.inc(i, "vv")
            P.block.vector(v_v)
            vv_v = P.cnt["vv"]

            def st_v(sync, dch=dch, sb=sb, vv_v=vv_v):
                sync.wait_ge(P.sems["vv"], vv_v)
                i = sync.dma_start(out=s1_d.rearrange("(c p) t -> p c t", p=128)[:, dch, sb],
                                   in_=ev[:, :])
                P.inc(i, "st", 16)
            P.block.sync(st_v)

    def fin(sync):
        sync.wait_ge(P.sems["st"], P.cnt["st"])
    P.block.sync(fin)
    return P.snap()


def emit_l1_ffn(P, cm):
    xnb_d = P.din("xn1b", [D, T], BF16)
    w1_d = P.din("l1_w1t", [D, 4096])
    w2_d = P.din("l1_w2t", [D, 4096])
    h1_d = P.din("l1_h1t", [4096, 4096])
    h2_d = P.din("l1_h2t", [4096, 4096])
    w3_d = P.din("l1_w3t", [4096, D])
    s1c_d = P.din("l1_s1", [128, 32])
    s2c_d = P.din("l1_s2", [128, 32])
    sh1_d = P.din("l1_sh1", [128, 32])
    sh2_d = P.din("l1_sh2", [128, 32])
    s3c_d = P.din("l1_s3", [128, NCH])
    x1_d = P.din("x1", [D, T])
    sat_d = P.din("s1", [D, T])
    x2_d = P.dout("x2", [D, T])
    h_d = P.nc.dram_tensor("l1_h", [4096, T], BF16, kind="Internal").ap()
    g1_d = P.nc.dram_tensor("l1_g1s", [4096, T], BF16, kind="Internal").ap()

    xnb = P.sb("g_xnb", [128, NCH, T], BF16)
    xtb = P.sb("g_xtb", [128, 32, 1024], BF16)
    wc1 = [P.sb("g_wc1a", [128, 32, 128], BF16), P.sb("g_wc1b", [128, 32, 128], BF16)]
    wc2 = [P.sb("g_wc2a", [128, 32, 128], BF16), P.sb("g_wc2b", [128, 32, 128], BF16)]
    scols = P.sb("g_scols", [128, 32 * 4 + NCH], F32)
    t1 = P.sb("g_t1", [128, 512], F32)
    t2 = P.sb("g_t2", [128, 512], F32)
    hcol = P.sb("g_hcol", [128, 512], BF16)
    xt = P.sb("g_xt", [128, 512], F32)
    s0t = P.sb("g_s0t", [128, 512], F32)
    ps1 = [P.psum("g_ps10", [128, 512], F32), P.psum("g_ps11", [128, 512], F32)]
    ps2 = [P.psum("g_ps20", [128, 512], F32), P.psum("g_ps21", [128, 512], F32)]
    s1c = scols[:, 0:32]
    s2c = scols[:, 32:64]
    sh1c = scols[:, 64:96]
    sh2c = scols[:, 96:128]
    s3c = scols[:, 128:136]

    def ld0(sync):
        for dst, src in [(s1c, s1c_d), (s2c, s2c_d), (sh1c, sh1_d), (sh2c, sh2_d), (s3c, s3c_d)]:
            i = sync.dma_start(out=dst, in_=src[:])
            P.inc(i, "ld", 16)
    P.block.sync(ld0)

    def ld0g(g):
        i = g.dma_start(out=xnb[:, :, :], in_=xnb_d.rearrange("(c p) t -> p c t", p=128))
        P.inc(i, "ld2", 16)
    P.block.gpsimd(ld0g)
    ld_done = P.snap()

    mmA = []
    for oc in range(32):
        gate = {"mm": (mmA[oc - 2] if oc >= 2 else 0), "vv": P.cnt["vv"] if oc < 2 else 0}

        def ldw(g, oc=oc, gate=gate, b=oc % 2):
            P.wait(g, gate)
            if oc == 0:
                P.wait(g, ld_done)
            i = g.dma_start(out=wc1[b][:, 0:NCH, :],
                            in_=w1_d.rearrange("(c p) o -> p c o", p=128)[:, :, 128 * oc:128 * (oc + 1)])
            P.inc(i, "ld2", 16)
            i = g.dma_start(out=wc2[b][:, 0:NCH, :],
                            in_=w2_d.rearrange("(c p) o -> p c o", p=128)[:, :, 128 * oc:128 * (oc + 1)])
            P.inc(i, "ld2", 16)
        P.block.gpsimd(ldw)
        ldn = P.cnt["ld2"]

        for s in range(NSL):
            sb = slice(512 * s, 512 * (s + 1))
            ssp = P.cnt["ss"]
            vvp = P.cnt["vv"]

            def t_a(tensor, oc=oc, s=s, sb=sb, ldn=ldn, ssp=ssp, vvp=vvp, b=oc % 2):
                tensor.wait_ge(P.sems["ld2"], ldn)
                P.wait(tensor, ld_done)
                tensor.wait_ge(P.sems["ss"], ssp)
                tensor.wait_ge(P.sems["vv"], vvp)
                for c in range(NCH):
                    tensor.matmul(ps1[s % 2][:, :], wc1[b][:, c, :], xnb[:, c, sb],
                                  start=(c == 0), stop=(c == NCH - 1))
                for c in range(NCH):
                    i = tensor.matmul(ps2[s % 2][:, :], wc2[b][:, c, :], xnb[:, c, sb],
                                      start=(c == 0), stop=(c == NCH - 1))
                P.inc(i, "mm")
            P.block.tensor(t_a)
            mm_a = P.cnt["mm"]
            stp = P.cnt["st"]

            def s_a(scalar, oc=oc, s=s, mm_a=mm_a, vvp=vvp):
                scalar.wait_ge(P.sems["mm"], mm_a)
                scalar.wait_ge(P.sems["vv"], vvp)
                scalar.activation(t1[:, :], ps1[s % 2][:, :], AF.Silu, scale=s1c[:, oc:oc + 1])
                i = scalar.activation(t2[:, :], ps2[s % 2][:, :], AF.Identity, scale=s2c[:, oc:oc + 1])
                P.inc(i, "ss")
            P.block.scalar(s_a)
            ss_a = P.cnt["ss"]

            def v_a(vector, ss_a=ss_a, stp=stp):
                vector.wait_ge(P.sems["ss"], ss_a)
                vector.wait_ge(P.sems["st"], stp)
                i = vector.tensor_tensor(hcol[:, :], t1[:, :], t2[:, :], ALU.mult)
                P.inc(i, "vv")
            P.block.vector(v_a)
            vvn = P.cnt["vv"]

            def st_a(sync, oc=oc, sb=sb, vvn=vvn):
                sync.wait_ge(P.sems["vv"], vvn)
                i = sync.dma_start(out=h_d.rearrange("(c p) t -> p c t", p=128)[:, oc, sb],
                                   in_=hcol[:, :])
                P.inc(i, "st", 16)
            P.block.sync(st_a)
        mmA.append(P.cnt["mm"])
    ph_a = P.snap()

    def big_phase(src_d, w_d, scol, dst_d, pre, tagname):
        for tb in range(2):
            tsl = slice(1024 * tb, 1024 * (tb + 1))
            prev = P.snap()

            def ld_x(sync, tsl=tsl, prev=prev, pre=pre):
                P.wait(sync, {"mm": prev["mm"], "st": pre["st"]})
                i = sync.dma_start(out=xtb[:, :, :],
                                   in_=src_d.rearrange("(c p) t -> p c t", p=128)[:, :, tsl])
                P.inc(i, "ld", 16)
            P.block.sync(ld_x)
            ld_x_n = P.snap()

            mmB = []
            for oc in range(32):
                gate = {"mm": (mmB[oc - 2] if oc >= 2 else 0)}

                def ldw(g, oc=oc, gate=gate, b=oc % 2):
                    P.wait(g, gate)
                    if oc == 0:
                        P.wait(g, ld_x_n)
                    i = g.dma_start(out=wc1[b][:, :, :],
                                    in_=w_d.rearrange("(c p) o -> p c o", p=128)[:, :, 128 * oc:128 * (oc + 1)])
                    P.inc(i, "ld2", 16)
                P.block.gpsimd(ldw)
                ldn = P.cnt["ld2"]

                for s in range(2):
                    sb = slice(512 * s, 512 * (s + 1))
                    gsb = slice(1024 * tb + 512 * s, 1024 * tb + 512 * (s + 1))
                    ssp = P.cnt["ss"]

                    def t_b(tensor, oc=oc, sb=sb, ldn=ldn, ssp=ssp, b=oc % 2, ld_x_n=ld_x_n):
                        tensor.wait_ge(P.sems["ld2"], ldn)
                        P.wait(tensor, ld_x_n)
                        tensor.wait_ge(P.sems["ss"], ssp)
                        for c in range(32):
                            i = tensor.matmul(ps1[s % 2][:, :], wc1[b][:, c, :], xtb[:, c, sb],
                                              start=(c == 0), stop=(c == 31))
                        P.inc(i, "mm")
                    P.block.tensor(t_b)
                    mm_b = P.cnt["mm"]
                    stp = P.cnt["st"]

                    def s_b(scalar, oc=oc, s=s, mm_b=mm_b, stp=stp):
                        scalar.wait_ge(P.sems["mm"], mm_b)
                        scalar.wait_ge(P.sems["st"], stp)
                        i = scalar.activation(hcol[:, :], ps1[s % 2][:, :], AF.Gelu,
                                              scale=scol[:, oc:oc + 1])
                        P.inc(i, "ss")
                    P.block.scalar(s_b)
                    ss_b = P.cnt["ss"]

                    def st_b(sync, oc=oc, gsb=gsb, ss_b=ss_b):
                        sync.wait_ge(P.sems["ss"], ss_b)
                        i = sync.dma_start(out=dst_d.rearrange("(c p) t -> p c t", p=128)[:, oc, gsb],
                                           in_=hcol[:, :])
                        P.inc(i, "st", 16)
                    P.block.sync(st_b)
                mmB.append(P.cnt["mm"])
        return P.snap()

    ph_b = big_phase(h_d, h1_d, sh1c, g1_d, ph_a, "b")
    ph_c = big_phase(g1_d, h2_d, sh2c, h_d, ph_b, "c")

    for tb in range(2):
        tsl = slice(1024 * tb, 1024 * (tb + 1))
        prev = P.snap()

        def ld_x(sync, tsl=tsl, prev=prev):
            P.wait(sync, {"mm": prev["mm"], "st": ph_c["st"]})
            i = sync.dma_start(out=xtb[:, :, :],
                               in_=h_d.rearrange("(c p) t -> p c t", p=128)[:, :, tsl])
            P.inc(i, "ld", 16)
        P.block.sync(ld_x)
        ld_x_n = P.snap()

        mmD = []
        for oc in range(NCH):
            gate = {"mm": (mmD[oc - 2] if oc >= 2 else 0)}

            def ldw(g, oc=oc, gate=gate, b=oc % 2):
                P.wait(g, gate)
                if oc == 0:
                    P.wait(g, ld_x_n)
                i = g.dma_start(out=wc1[b][:, :, :],
                                in_=w3_d.rearrange("(c p) o -> p c o", p=128)[:, :, 128 * oc:128 * (oc + 1)])
                P.inc(i, "ld2", 16)
            P.block.gpsimd(ldw)
            ldn = P.cnt["ld2"]

            for s in range(2):
                sb = slice(512 * s, 512 * (s + 1))
                gsb = slice(1024 * tb + 512 * s, 1024 * tb + 512 * (s + 1))
                ssp = P.cnt["ss"]
                vvp = P.cnt["vv"]

                def ld_rs(g, oc=oc, gsb=gsb, vvp=vvp):
                    P.wait(g, {"vv": vvp})
                    i = g.dma_start(out=xt[:, :], in_=x1_d.rearrange("(c p) t -> p c t", p=128)[:, oc, gsb])
                    P.inc(i, "ld2", 16)
                    i = g.dma_start(out=s0t[:, :], in_=sat_d.rearrange("(c p) t -> p c t", p=128)[:, oc, gsb])
                    P.inc(i, "ld2", 16)
                P.block.gpsimd(ld_rs)
                ld_rs_n = P.cnt["ld2"]

                def t_d(tensor, oc=oc, sb=sb, ldn=ldn, ssp=ssp, b=oc % 2, ld_x_n=ld_x_n):
                    tensor.wait_ge(P.sems["ld2"], ldn)
                    P.wait(tensor, ld_x_n)
                    tensor.wait_ge(P.sems["ss"], ssp)
                    for c in range(32):
                        i = tensor.matmul(ps1[s % 2][:, :], wc1[b][:, c, :], xtb[:, c, sb],
                                          start=(c == 0), stop=(c == 31))
                    P.inc(i, "mm")
                P.block.tensor(t_d)
                mm_d = P.cnt["mm"]

                def s_d(scalar, oc=oc, s=s, mm_d=mm_d, vvp=vvp):
                    scalar.wait_ge(P.sems["mm"], mm_d)
                    scalar.wait_ge(P.sems["vv"], vvp)
                    i = scalar.activation(t1[:, :], ps1[s % 2][:, :],
                                          AF.Silu, scale=s3c[:, oc:oc + 1])
                    P.inc(i, "ss")
                P.block.scalar(s_d)
                ss_d = P.cnt["ss"]
                stp = P.cnt["st"]

                def v_d(vector, ss_d=ss_d, stp=stp, ld_rs_n=ld_rs_n):
                    vector.wait_ge(P.sems["ss"], ss_d)
                    vector.wait_ge(P.sems["ld2"], ld_rs_n)
                    vector.wait_ge(P.sems["st"], stp)
                    vector.tensor_tensor(t2[:, :], t1[:, :], s0t[:, :], ALU.mult)
                    i = vector.tensor_tensor(xt[:, :], xt[:, :], t2[:, :], ALU.add)
                    P.inc(i, "vv")
                P.block.vector(v_d)
                vv_d = P.cnt["vv"]

                def st_d(sync, oc=oc, gsb=gsb, vv_d=vv_d):
                    sync.wait_ge(P.sems["vv"], vv_d)
                    i = sync.dma_start(out=x2_d.rearrange("(c p) t -> p c t", p=128)[:, oc, gsb],
                                       in_=xt[:, :])
                    P.inc(i, "st", 16)
                P.block.sync(st_d)
            mmD.append(P.cnt["mm"])

    def fin(sync):
        sync.wait_ge(P.sems["st"], P.cnt["st"])
    P.block.sync(fin)
    return P.snap()


def emit_ln2x(P, cm):
    x_d = P.din("x2", [D, T])
    g_d = P.din("l2f_g", [128, NCH])
    b_d = P.din("l2f_b", [128, NCH])
    xn_b_d = P.dout("xn2b", [D, T], BF16)
    xn_f_d = P.dout("xn2f", [D, T])
    xq_d = P.dout("xq_n", [D, T])
    gm_d = P.dout("gmax_loc", [128, NCH])

    xbuf = P.sb("g1_x", [128, NCH, T], F32)
    xb16 = P.sb("g1_xb", [128, NCH, T], BF16)
    xq = P.sb("g1_xq", [128, NCH, T], F32)
    gcol = P.sb("g1_g", [128, NCH], F32)
    bcol = P.sb("g1_b", [128, NCH], F32)
    gm = P.sb("g1_gm", [128, NCH], F32)
    scr = mk_ln_scratch(P, "g1")
    ld = load_fm(P, x_d, xbuf)

    def ldgb(sync):
        i = sync.dma_start(out=gcol[:, :], in_=g_d[:, :])
        P.inc(i, "ld", 16)
        i = sync.dma_start(out=bcol[:, :], in_=b_d[:, :])
        P.inc(i, "ld", 16)
    P.block.sync(ldgb)
    ld2 = P.snap()

    done = ln_fm(P, cm, xbuf, xbuf, scr, eps=1e-5, pre=ld)
    store_fm(P, xbuf, xn_f_d, done)
    cv = conv_bf16(P, xbuf, xb16, pre=done, eng="vector", sem="vv")
    store_fm(P, xb16, xn_b_d, cv)
    done2 = ln_fm(P, cm, xbuf, xq, scr, eps=1e-3, gamma=gcol, beta=bcol, pre=snap_max(cv, ld2))
    store_fm(P, xq, xq_d, done2)

    def v_gm(vector):
        P.wait(vector, done2)
        for c in range(NCH):
            i = vector.tensor_reduce(gm[:, c:c + 1], xq[:, c, :], AX.X, ALU.max,
                                     apply_absolute_value=True)
        P.inc(i, "vv")
    P.block.vector(v_gm)
    vvn = P.cnt["vv"]

    def st_gm(sync):
        sync.wait_ge(P.sems["vv"], vvn)
        i = sync.dma_start(out=gm_d[:, :], in_=gm[:, :])
        P.inc(i, "st", 16)
        sync.wait_ge(P.sems["st"], P.cnt["st"])
    P.block.sync(st_gm)
    return P.snap()


def emit_l2a(P, cm):
    xnb_d = P.din("xn2b", [D, T], BF16)
    lnw_d = P.din("l2_lnw", [128, NCH])
    lnb_d = P.din("l2_lnb", [128, NCH])
    wh_d = P.din("l2_wh", [D, 4096])
    bh_d = P.din("l2_bh", [128, 32])
    wqk_d = P.din("l2_wqk", [D, 128])
    bqk_d = P.din("l2_bqk", [128, 1])
    qkw_d = P.din("l2_qkw", [128, 4])
    qkb_d = P.din("l2_qkb", [128, 4])
    tbl_d = P.din("l2_tblr", [1, 32])
    bkt_d = P.din("l2_bkt", [128, 2, 256])
    gate_d = P.dout("l2_gate", [2048, T], BF16)
    quad_d = P.dout("l2_quad", [2048, T], BF16)
    lq_d = P.dout("l2_lq", [128, T], BF16)
    kvg_d = P.dout("l2_kvg", [NCH, 128, 2048])
    kvo_d = P.dout("kv_own", [128, 2048])
    bts_d = P.nc.dram_tensor("l2_bts", [768], F32, kind="Internal").ap()

    xnb = P.sb("g2_xn", [128, NCH, T], BF16)
    nrm = xnb
    v2 = P.sb("g2_v2", [128, 16, T], BF16)
    qk = P.sb("g2_qk", [128, T], F32)
    qq = P.sb("g2_qq", [128, T], BF16)
    qkq = P.sb("g2_qkq", [128, T], BF16)
    lq = P.sb("g2_lq", [128, T], BF16)
    lk = P.sb("g2_lk", [128, T], BF16)
    vg = P.sb("g2_vg", [128, 2, 2048], BF16)
    lkt = P.sb("g2_lkt", [128, 2, 128], BF16)
    att = P.sb("g2_att", [128, 2, 256], BF16)
    bT = P.sb("g2_bT", [128, 2, 256], F32)
    simt = P.sb("g2_simt", [128, 256], F32)
    tblr = P.sb("g2_tblr", [1, 32], F32)
    tblb = P.sb("g2_tblb", [128, 32], F32)
    bkt = P.sb("g2_bkt", [128, 2, 256], F32)
    kvg = P.sb("g2_kvg", [128, 2048], F32)
    kvo = P.sb("g2_kvo", [128, 2048], F32)
    cols = P.sb("g2_cols", [128, 64], F32)
    wcb = P.sb("g2_wcb", [128, NCH, 128], F32)
    wcb16 = P.sb("g2_wcb16", [128, NCH, 128], BF16)
    wcb16b = P.sb("g2_wcb16b", [128, NCH, 128], BF16)
    ev = P.sb("g2_ev", [128, 256], BF16)
    ps = [P.psum("g2_ps0", [128, 512], F32), P.psum("g2_ps1", [128, 512], F32)]
    scr = mk_ln_scratch(P, "g2", ps_a=ps[0], ps_b=ps[1])
    psT = [P.psum("g2_psT0", [128, 128], BF16), P.psum("g2_psT1", [128, 128], BF16)]
    psq = P.psum("g2_psq", [128, 256], F32)
    lnw = cols[:, 0:NCH]
    lnb = cols[:, 8:16]
    bh = cols[:, 16:48]
    bqk = cols[:, 48:49]
    qkw = cols[:, 49:53]
    qkb = cols[:, 53:57]

    def ld0(sync):
        for dst, src in [(lnw, lnw_d), (lnb, lnb_d), (bh, bh_d), (bqk, bqk_d),
                         (qkw, qkw_d), (qkb, qkb_d), (tblr[:, :], tbl_d), (bkt[:, :, :], bkt_d)]:
            i = sync.dma_start(out=dst, in_=src[:])
            P.inc(i, "ld", 16)
    P.block.sync(ld0)

    def ld0g(g):
        i = g.dma_start(out=xnb[:, :, :], in_=xnb_d.rearrange("(c p) t -> p c t", p=128))
        P.inc(i, "ld2", 16)
    P.block.gpsimd(ld0g)
    ld_done = P.snap()

    done = ln_fm(P, cm, xnb, nrm, scr, eps=1e-5, gamma=lnw, beta=lnb, pre=ld_done)

    def t_rb(tensor, done=done):
        P.wait(tensor, ld_done)
        i = tensor.matmul(psq[:, 0:32], cm.ones_rowf[:, :], tblr[:, :], start=True, stop=True)
        P.inc(i, "mm")
    P.block.tensor(t_rb)
    mm_rb = P.cnt["mm"]

    def v_rb(vector, mm_rb=mm_rb):
        vector.wait_ge(P.sems["mm"], mm_rb)
        vector.tensor_scalar(tblb[:, :], psq[:, 0:32], 128.0 ** 0.5, None, ALU.mult)
        vector.memset(bT[:, :, :], 0.0)
        vgap(cm, vector)
        for b in range(32):
            vector.tensor_single_scalar(simt[:, 0:256], bkt[:, 0, :], float(b), ALU.is_equal)
            vector.scalar_tensor_tensor(bT[:, 0, :], simt[:, 0:256], tblb[:, b:b + 1],
                                        bT[:, 0, :], ALU.mult, ALU.add)
            vector.tensor_single_scalar(simt[:, 0:256], bkt[:, 1, :], float(b), ALU.is_equal)
            vector.scalar_tensor_tensor(bT[:, 1, :], simt[:, 0:256], tblb[:, b:b + 1],
                                        bT[:, 1, :], ALU.mult, ALU.add)
        i = vector.engine_nop()
        P.inc(i, "vv")
    P.block.vector(v_rb)
    bT_done = P.snap()

    ev2 = P.sb("g2_ev2", [128, 512], BF16)

    def evac_hv(n, oc, sb, mmn):
        stp = P.cnt["st"]

        def s_e(scalar, n=n, oc=oc, sb=sb, mmn=mmn, stp=stp):
            scalar.wait_ge(P.sems["mm"], mmn)
            if oc < 16:
                i = scalar.activation(v2[:, oc, sb], ps[n % 2][:, :], AF.Silu,
                                      bias=bh[:, oc:oc + 1])
            else:
                scalar.wait_ge(P.sems["st"], stp)
                i = scalar.activation(ev2[:, :], ps[n % 2][:, :], AF.Silu,
                                      bias=bh[:, oc:oc + 1])
            P.inc(i, "ss")
        P.block.scalar(s_e)
        if oc >= 16:
            ssn = P.cnt["ss"]

            def st_e(sync, oc=oc, sb=sb, ssn=ssn):
                sync.wait_ge(P.sems["ss"], ssn)
                i = sync.dma_start(out=gate_d.rearrange("(c p) t -> p c t", p=128)[:, oc - 16, sb],
                                   in_=ev2[:, :])
                P.inc(i, "st", 16)
            P.block.sync(st_e)

    def wview_h(oc):
        return wh_d.rearrange("(c p) o -> p c o", p=128)[:, :, 128 * oc:128 * (oc + 1)]

    mm_stream(P, cm, wview_h, 32, nrm, NCH, ps, evac_hv, "g2h", wcb, wcb16,
              pre=done, evac_sem="ss", cast_dma=True, wcb2=wcb, wcb16_2=wcb16b)
    hv_done = P.snap()

    def evac_qk(n, oc, sb, mmn):
        def s_e(scalar, n=n, sb=sb, mmn=mmn):
            scalar.wait_ge(P.sems["mm"], mmn)
            i = scalar.activation(qk[:, sb], ps[n % 2][:, :], AF.Silu, bias=bqk[:, 0:1])
            P.inc(i, "ss")
        P.block.scalar(s_e)

    def wview_qk(oc):
        return wqk_d.rearrange("(c p) o -> p c o", p=128)[:, :, 0:128]

    mm_stream(P, cm, wview_qk, 1, nrm, NCH, ps, evac_qk, "g2qk", wcb, wcb16,
              pre=hv_done, evac_sem="ss")
    qk_done = P.snap()

    def v_qk4(vector):
        P.wait(vector, qk_done)
        vector.tensor_scalar(qq[:, :], qk[:, :], qkw[:, 0:1], qkb[:, 0:1], ALU.mult, ALU.add)
        vector.tensor_scalar(lq[:, :], qk[:, :], qkw[:, 1:2], qkb[:, 1:2], ALU.mult, ALU.add)
        vector.tensor_scalar(qkq[:, :], qk[:, :], qkw[:, 2:3], qkb[:, 2:3], ALU.mult, ALU.add)
        i = vector.tensor_scalar(lk[:, :], qk[:, :], qkw[:, 3:4], qkb[:, 3:4], ALU.mult, ALU.add)
        P.inc(i, "vv")
    P.block.vector(v_qk4)
    qk4_done = P.snap()

    def st_lq(sync, w=qk4_done):
        P.wait(sync, w)
        i = sync.dma_start(out=lq_d.rearrange("(a p) t -> p a t", p=128), in_=lq[:, None, :])
        P.inc(i, "st", 16)
    P.block.sync(st_lq)

    def g_z(g):
        i = g.memset(kvo[:, :], 0.0)
        P.inc(i, "gp")
    P.block.gpsimd(g_z)
    kvo_zero = P.snap()

    for g in range(NCH):
        gsl = slice(256 * g, 256 * (g + 1))
        prev_g = P.snap()

        mm0 = P.cnt["mm"]
        vv0 = P.cnt["vv"]

        def t_tr(tensor, g=g, prev_g=prev_g, mm0=mm0, vv0=vv0):
            P.wait(tensor, {"vv": prev_g["vv"], "ss": prev_g["ss"], "st": prev_g["st"]})
            n = 0
            for ec in range(16):
                for jt in range(2):
                    if n >= 2:
                        tensor.wait_ge(P.sems["vv"], vv0 + n - 1)
                    i = tensor.transpose(psT[n % 2][:, :],
                                         v2[:, ec, 256 * g + 128 * jt:256 * g + 128 * (jt + 1)],
                                         cm.identb[:])
                    P.inc(i, "mm")
                    n += 1
            for jt in range(2):
                tensor.wait_ge(P.sems["vv"], vv0 + n - 1)
                i = tensor.transpose(psT[n % 2][:, :],
                                     lk[:, 256 * g + 128 * jt:256 * g + 128 * (jt + 1)],
                                     cm.identb[:])
                P.inc(i, "mm")
                n += 1
        def v_tr(vector, g=g, mm0=mm0):
            n = 0
            for ec in range(16):
                for jt in range(2):
                    vector.wait_ge(P.sems["mm"], mm0 + n + 1)
                    i = vector.tensor_copy(vg[:, jt, 128 * ec:128 * (ec + 1)], psT[n % 2][:, :])
                    P.inc(i, "vv")
                    n += 1
            for jt in range(2):
                vector.wait_ge(P.sems["mm"], mm0 + n + 1)
                i = vector.tensor_copy(lkt[:, jt, :], psT[n % 2][:, :])
                P.inc(i, "vv")
                n += 1
        P.block.tensor(t_tr)
        P.block.vector(v_tr)
        tr_done = P.snap()

        for jt in range(2):
            mmq0 = P.cnt["mm"]
            vvq = P.cnt["vv"]
            ssq = P.cnt["ss"]

            def t_q(tensor, g=g, jt=jt, tr_done=tr_done, vvq=vvq):
                P.wait(tensor, {"vv": tr_done["vv"]})
                P.wait(tensor, bT_done)
                tensor.wait_ge(P.sems["vv"], vvq)
                i = tensor.matmul(psq[:, :], qkq[:, 256 * g + 128 * jt:256 * g + 128 * (jt + 1)],
                                  qq[:, gsl], start=True, stop=True)
                P.inc(i, "mm")
            P.block.tensor(t_q)
            mm_q = P.cnt["mm"]

            def v_q(vector, jt=jt, mm_q=mm_q, ssq=ssq):
                vector.wait_ge(P.sems["mm"], mm_q)
                vector.wait_ge(P.sems["ss"], ssq)
                vector.tensor_scalar(simt[:, :], psq[:, :], 1.0 / 256, None, ALU.mult)
                vector.tensor_tensor(simt[:, :], simt[:, :], bT[:, jt, :], ALU.add)
                i = vector.tensor_scalar(simt[:, :], simt[:, :], 0.0, None, ALU.max)
                P.inc(i, "vv")
            P.block.vector(v_q)
            vv_q = P.cnt["vv"]

            def s_q(scalar, jt=jt, vv_q=vv_q):
                scalar.wait_ge(P.sems["vv"], vv_q)
                i = scalar.activation(att[:, jt, :], simt[:, :], AF.Square)
                P.inc(i, "ss")
            P.block.scalar(s_q)
            ss_q = P.cnt["ss"]

            def g_m(gp_, jt=jt, ss_q=ss_q):
                gp_.wait_ge(P.sems["ss"], ss_q)
                i = gp_.affine_select(out=att[:, jt, :], in_=att[:, jt, :],
                                      compare_op=ALU.is_ge, fill=0.0,
                                      base=-128 * jt, pattern=[[1, 256]], channel_multiplier=-1)
                P.inc(i, "gp")
            P.block.gpsimd(g_m)
        att_done = P.snap()

        for ec in range(16):
            vvp = P.cnt["vv"]
            ssp = P.cnt["ss"]

            def t_qo(tensor, g=g, ec=ec, att_done=att_done, ssp=ssp):
                P.wait(tensor, {"gp": att_done["gp"], "vv": att_done["vv"]})
                tensor.wait_ge(P.sems["ss"], ssp)
                for jt in range(2):
                    i = tensor.matmul(ps[ec % 2][:, 0:256], vg[:, jt, 128 * ec:128 * (ec + 1)],
                                      att[:, jt, :], start=(jt == 0), stop=(jt == 1))
                P.inc(i, "mm")
            P.block.tensor(t_qo)
            mm_qo = P.cnt["mm"]
            stp = P.cnt["st"]

            def s_qo(scalar, ec=ec, mm_qo=mm_qo, stp=stp):
                scalar.wait_ge(P.sems["mm"], mm_qo)
                scalar.wait_ge(P.sems["st"], stp)
                i = scalar.activation(ev[:, :], ps[ec % 2][:, 0:256], AF.Identity)
                P.inc(i, "ss")
            P.block.scalar(s_qo)
            ss_qo = P.cnt["ss"]

            def st_qo(sync, g=g, ec=ec, ss_qo=ss_qo):
                sync.wait_ge(P.sems["ss"], ss_qo)
                i = sync.dma_start(out=quad_d.rearrange("(c p) t -> p c t", p=128)[:, ec, gsl],
                                   in_=ev[:, :])
                P.inc(i, "st", 16)
            P.block.sync(st_qo)

        for eq in range(4):
            esl = slice(512 * eq, 512 * (eq + 1))
            vvp = P.cnt["vv"]

            def t_kv(tensor, g=g, eq=eq, esl=esl, vvp=vvp, ssp2=P.cnt["ss"], att_done=att_done):
                P.wait(tensor, {"vv": att_done["vv"]})
                tensor.wait_ge(P.sems["vv"], vvp)
                tensor.wait_ge(P.sems["ss"], ssp2)
                for jt in range(2):
                    i = tensor.matmul(ps[eq % 2][:, :], lkt[:, jt, :], vg[:, jt, esl],
                                      start=(jt == 0), stop=(jt == 1))
                P.inc(i, "mm")
            P.block.tensor(t_kv)
            mm_kv = P.cnt["mm"]

            def v_kv(vector, eq=eq, esl=esl, mm_kv=mm_kv):
                vector.wait_ge(P.sems["mm"], mm_kv)
                vector.tensor_scalar(kvg[:, esl], ps[eq % 2][:, :], 1.0 / 256, None, ALU.mult)
                i = vector.tensor_tensor(kvo[:, esl], kvo[:, esl], kvg[:, esl], ALU.add)
                P.inc(i, "vv")
            P.block.vector(v_kv)
        kv_done = P.snap()

        def st_kv(sync, g=g, w=kv_done):
            P.wait(sync, w)
            i = sync.dma_start(out=kvg_d[g, :, :], in_=kvg[:, :])
            P.inc(i, "st", 16)
        P.block.sync(st_kv)

    def st_kvo(sync, w=P.snap()):
        P.wait(sync, w)
        i = sync.dma_start(out=kvo_d[:, :], in_=kvo[:, :])
        P.inc(i, "st", 16)
        sync.wait_ge(P.sems["st"], P.cnt["st"])
    P.block.sync(st_kvo)
    return P.snap()


def emit_l2b(P, cm):
    kvg_d = P.din("l2_kvg", [NCH, 128, 2048])
    kvp_d = P.din("kv_prev", [128, 2048])
    lq_d = P.din("l2_lq", [128, T], BF16)
    gate_d = P.din("l2_gate", [2048, T], BF16)
    quad_d = P.din("l2_quad", [2048, T], BF16)
    wo_d = P.din("l2_wout", [2048, D])
    bo_d = P.din("l2_bout", [128, NCH])
    xn_d = P.din("xn2f", [D, T])
    s2_d = P.dout("s2", [D, T])

    kvacc = P.sb("g3_kv", [128, 2048], F32)
    kvt = P.sb("g3_kvt", [128, 2048], F32)
    kvb = P.sb("g3_kvb", [128, 16, 128], BF16)
    lq = P.sb("g3_lq", [128, T], BF16)
    og = P.sb("g3_og", [128, 16, T], BF16)
    gt = P.sb("g3_gt", [128, 512], BF16)
    qt = P.sb("g3_qt", [128, 512], BF16)
    ev = P.sb("g3_ev", [128, 512], F32)
    xt = P.sb("g3_xt", [128, 512], F32)
    bo = P.sb("g3_bo", [128, NCH], F32)
    wcb = P.sb("g3_wcb", [128, 16, 128], F32)
    wcb16 = P.sb("g3_wcb16", [128, 16, 128], BF16)
    wcb16b = P.sb("g3_wcb16b", [128, 16, 128], BF16)
    ps = [P.psum("g3_ps0", [128, 512], F32), P.psum("g3_ps1", [128, 512], F32)]

    def ld0(sync):
        i = sync.dma_start(out=kvacc[:, :], in_=kvp_d[:, :])
        P.inc(i, "ld", 16)
        i = sync.dma_start(out=bo[:, :], in_=bo_d[:, :])
        P.inc(i, "ld", 16)
    P.block.sync(ld0)

    def ld0g(g):
        i = g.dma_start(out=lq[:, :], in_=lq_d.rearrange("(a p) t -> p a t", p=128)[:, 0, :])
        P.inc(i, "ld2", 16)
    P.block.gpsimd(ld0g)
    ld_done = P.snap()

    for g in range(NCH):
        gsl = slice(256 * g, 256 * (g + 1))
        prev_g = P.snap()

        def v_kvb(vector, prev_g=prev_g):
            P.wait(vector, ld_done)
            P.wait(vector, {"mm": prev_g["mm"]})
            i = vector.tensor_copy(kvb[:, :, :], kvacc[:, :].rearrange("p (a b) -> p a b", a=16))
            P.inc(i, "vv")
        P.block.vector(v_kvb)
        kvb_done = P.cnt["vv"]

        def ld_kv(sync, g=g, prev_g=prev_g):
            P.wait(sync, {"vv": P.cnt["vv"]})
            i = sync.dma_start(out=kvt[:, :], in_=kvg_d[g, :, :])
            P.inc(i, "ld", 16)
        P.block.sync(ld_kv)
        ld_kv_n = P.cnt["ld"]

        for ec in range(16):
            vvp = P.cnt["vv"]
            ssp = P.cnt["ss"]

            def t_lo(tensor, g=g, ec=ec, gsl=gsl, kvb_done=kvb_done, vvb=P.cnt["vv"]):
                tensor.wait_ge(P.sems["vv"], max(kvb_done, vvb))
                i = tensor.matmul(ps[ec % 2][:, 0:256], kvb[:, ec, :], lq[:, gsl],
                                  start=True, stop=True)
                P.inc(i, "mm")
            P.block.tensor(t_lo)
            mm_lo = P.cnt["mm"]
            ld2p = P.cnt["ld2"]

            def ld_gq(gp_, g=g, ec=ec, gsl=gsl, vvp=vvp):
                P.wait(gp_, {"vv": vvp})
                i = gp_.dma_start(out=gt[:, 0:256], in_=gate_d.rearrange("(c p) t -> p c t", p=128)[:, ec, gsl])
                P.inc(i, "ld2", 16)
                i = gp_.dma_start(out=qt[:, 0:256], in_=quad_d.rearrange("(c p) t -> p c t", p=128)[:, ec, gsl])
                P.inc(i, "ld2", 16)
            P.block.gpsimd(ld_gq)
            ld_gq_n = P.cnt["ld2"]

            def v_og(vector, g=g, ec=ec, gsl=gsl, mm_lo=mm_lo, ld_gq_n=ld_gq_n):
                vector.wait_ge(P.sems["mm"], mm_lo)
                vector.wait_ge(P.sems["ld2"], ld_gq_n)
                vector.tensor_tensor(ev[:, 0:256], qt[:, 0:256], ps[ec % 2][:, 0:256], ALU.add)
                i = vector.tensor_tensor(og[:, ec, gsl], ev[:, 0:256], gt[:, 0:256], ALU.mult)
                P.inc(i, "vv")
            P.block.vector(v_og)

        lo_done = P.snap()

        def v_acc(vector, g=g, lo_done=lo_done, ld_kv_n=ld_kv_n):
            vector.wait_ge(P.sems["ld"], ld_kv_n)
            i = vector.tensor_tensor(kvacc[:, :], kvacc[:, :], kvt[:, :], ALU.add)
            P.inc(i, "vv")
        P.block.vector(v_acc)
    og_done = P.snap()

    def evac_s2(n, oc, sb, mmn):
        ld2p = P.cnt["ld2"]

        def ld_x(gp_, oc=oc, sb=sb, n=n):
            P.wait(gp_, {"vv": P.cnt["vv"]})
            i = gp_.dma_start(out=xt[:, :], in_=xn_d.rearrange("(c p) t -> p c t", p=128)[:, oc, sb])
            P.inc(i, "ld2", 16)
        P.block.gpsimd(ld_x)
        ld_n = P.cnt["ld2"]

        def s_e(scalar, n=n, oc=oc, mmn=mmn):
            scalar.wait_ge(P.sems["mm"], mmn)
            scalar.wait_ge(P.sems["vv"], P.cnt["vv"])
            i = scalar.activation(ev[:, :], ps[n % 2][:, :], AF.Identity, bias=bo[:, oc:oc + 1])
            P.inc(i, "ss")
        P.block.scalar(s_e)
        ssn = P.cnt["ss"]
        stp = P.cnt["st"]

        def v_e(vector, oc=oc, sb=sb, ssn=ssn, stp=stp, ld_n=ld_n):
            vector.wait_ge(P.sems["ss"], ssn)
            vector.wait_ge(P.sems["ld2"], ld_n)
            vector.wait_ge(P.sems["st"], stp)
            i = vector.tensor_tensor(xt[:, :], xt[:, :], ev[:, :], ALU.add)
            P.inc(i, "vv")
        P.block.vector(v_e)
        vvn = P.cnt["vv"]

        def st_e(sync, oc=oc, sb=sb, vvn=vvn):
            sync.wait_ge(P.sems["vv"], vvn)
            i = sync.dma_start(out=s2_d.rearrange("(c p) t -> p c t", p=128)[:, oc, sb],
                               in_=xt[:, :])
            P.inc(i, "st", 16)
        P.block.sync(st_e)

    def wview(oc):
        return wo_d.rearrange("(c p) o -> p c o", p=128)[:, :, 128 * oc:128 * (oc + 1)]

    mm_stream(P, cm, wview, NCH, og, 16, ps, evac_s2, "g3o", wcb, wcb16,
              pre=og_done, evac_sem="vv", cast_dma=True, wcb2=wcb, wcb16_2=wcb16b)

    def fin(sync):
        sync.wait_ge(P.sems["st"], P.cnt["st"])
    P.block.sync(fin)
    return P.snap()


def emit_l2ffn(P, cm, dbg=False):
    xq_d = P.din("xq_n", [D, T])
    gm_d = P.din("gmax_all", [1, 8 * 128 * NCH])
    w_d = P.din("l2f_w", [D, D])
    x2_d = P.din("x2", [D, T])
    s2_d = P.din("s2", [D, T])
    x3_d = P.dout("x3", [D, T])

    xq = P.sb("g4_xq", [128, NCH, T], F32)
    xqb = P.sb("g4_xqb", [128, NCH, T], BF16)
    wf = P.sb("g4_wf", [128, NCH, D], F32)
    wq16 = P.sb("g4_wq", [128, NCH, D], BF16)
    grow = P.sb("g4_grow", [1, 8 * 128 * NCH], F32)
    sc = P.sb("g4_sc", [1, 32], F32)
    red = P.sb("g4_red", [128, NCH], F32)
    colg = P.sb("g4_colg", [128, 4], F32)
    ev = P.sb("g4_ev", [128, 512], F32)
    xt = P.sb("g4_xt", [128, 512], F32)
    s0t = P.sb("g4_s0t", [128, 512], F32)
    ps = [P.psum("g4_ps0", [128, 512], F32), P.psum("g4_ps1", [128, 512], F32)]
    ps_c = P.psum("g4_psc", [1, 512], F32)
    ps_g = P.psum("g4_psg", [128, 1], F32)

    ld = load_fm(P, xq_d, xq)

    def ld0(sync):
        i = sync.dma_start(out=grow[:, :], in_=gm_d[:, :])
        P.inc(i, "ld", 16)
        i = sync.dma_start(out=wf[:, :, :], in_=w_d.rearrange("(c p) o -> p c o", p=128))
        P.inc(i, "ld", 16)
    P.block.sync(ld0)
    ld_done = P.snap()

    redt = P.sb("g4_redt", [128, 1], F32)

    def v_bt(vector):
        P.wait(vector, ld_done)
        for c in range(NCH):
            i = vector.tensor_reduce(red[:, c:c + 1], wf[:, c, :], AX.X, ALU.add,
                                     apply_absolute_value=True)
        vgap(cm, vector)
        vgap(cm, vector)
        i = vector.tensor_reduce(redt[:, 0:1], red[:, :], AX.X, ALU.add)
        P.inc(i, "vv")
    P.block.vector(v_bt)
    vv_bt = P.cnt["vv"]

    def t_bt(tensor, vv_bt=vv_bt):
        tensor.wait_ge(P.sems["vv"], vv_bt)
        i = tensor.matmul(ps_g[0:1, 0:1], cm.ones_colf[:, 0:1],
                          redt[:, 0:1], start=True, stop=True)
        P.inc(i, "mm")
    P.block.tensor(t_bt)
    mm_bt = P.cnt["mm"]

    def v_sc(vector, mm_bt=mm_bt):
        vector.wait_ge(P.sems["mm"], mm_bt)
        vector.reduce_max(sc[:, 0:1], grow[:, :], axis=AX.X)
        vector.tensor_scalar(sc[:, 1:2], ps_g[0:1, 0:1], 1.0 / (1024.0 * 1024.0), None, ALU.mult)
        vgap(cm, vector)
        vgap(cm, vector)
        vector.tensor_tensor(sc[:, 8:9], sc[:, 0:1], sc[:, 1:2], ALU.mult)
        vgap(cm, vector)
        vgap(cm, vector)
        vector.tensor_scalar(sc[:, 2:3], sc[:, 8:9], 1.0 / 128.0, None, ALU.mult)
        vector.tensor_copy(sc[:, 3:4], sc[:, 0:1])
        vgap(cm, vector)
        vgap(cm, vector)
        vector.reciprocal(sc[:, 9:10], sc[:, 3:4])
        vgap(cm, vector)
        vgap(cm, vector)
        vector.tensor_scalar(sc[:, 3:4], sc[:, 9:10], 128.0, None, ALU.mult)
        vector.tensor_scalar(sc[:, 10:11], sc[:, 1:2], 1e-5, None, ALU.add)
        vgap(cm, vector)
        vgap(cm, vector)
        vector.reciprocal(sc[:, 4:5], sc[:, 10:11])
        vgap(cm, vector)
        vgap(cm, vector)
        i = vector.engine_nop()
        P.inc(i, "vv")
    P.block.vector(v_sc)
    vv_sc = P.cnt["vv"]

    def t_bc(tensor, vv_sc=vv_sc):
        tensor.wait_ge(P.sems["vv"], vv_sc)
        i = tensor.matmul(ps_g[:, 0:1], cm.ones_rowf[:, :], sc[:, 3:4], start=True, stop=True)
        P.inc(i, "mm")
    P.block.tensor(t_bc)
    mm_bc = P.cnt["mm"]

    def v_bc(vector, mm_bc=mm_bc):
        vector.wait_ge(P.sems["mm"], mm_bc)
        vector.tensor_copy(colg[:, 0:1], ps_g[:, 0:1])
        i = vector.engine_nop()
        P.inc(i, "vv")
    P.block.vector(v_bc)
    vv_bc0 = P.cnt["vv"]

    def t_bc2(tensor, vv_bc0=vv_bc0):
        tensor.wait_ge(P.sems["vv"], vv_bc0)
        i = tensor.matmul(ps_g[:, 0:1], cm.ones_rowf[:, :], sc[:, 2:3], start=True, stop=True)
        P.inc(i, "mm")
    P.block.tensor(t_bc2)
    mm_bc2 = P.cnt["mm"]

    def v_bc2(vector, mm_bc2=mm_bc2):
        vector.wait_ge(P.sems["mm"], mm_bc2)
        i = vector.tensor_copy(colg[:, 1:2], ps_g[:, 0:1])
        P.inc(i, "vv")
    P.block.vector(v_bc2)
    vv_bc2 = P.cnt["vv"]

    def t_bc3(tensor, vv_bc2=vv_bc2):
        tensor.wait_ge(P.sems["vv"], vv_bc2)
        i = tensor.matmul(ps_g[:, 0:1], cm.ones_rowf[:, :], sc[:, 4:5], start=True, stop=True)
        P.inc(i, "mm")
    P.block.tensor(t_bc3)
    mm_bc3 = P.cnt["mm"]

    def v_bc3(vector, mm_bc3=mm_bc3):
        vector.wait_ge(P.sems["mm"], mm_bc3)
        i = vector.tensor_copy(colg[:, 2:3], ps_g[:, 0:1])
        P.inc(i, "vv")
    P.block.vector(v_bc3)
    bc_done = P.snap()

    MAGIC = 12582912.0

    def v_wq(vector, bc_done=bc_done):
        P.wait(vector, bc_done)
        for c in range(NCH):
            vector.tensor_scalar(wf[:, c, :], wf[:, c, :], colg[:, 2:3], None, ALU.mult)
            vector.tensor_scalar(wf[:, c, :], wf[:, c, :], MAGIC, -MAGIC, ALU.add, ALU.add)
            i = vector.tensor_scalar(wq16[:, c, :], wf[:, c, :], -1.0, 1.0, ALU.max, ALU.min)
            P.inc(i, "vv")
    P.block.vector(v_wq)

    def v_xq(vector, bc_done=bc_done):
        P.wait(vector, bc_done)
        for c in range(NCH):
            vector.tensor_scalar(xq[:, c, :], xq[:, c, :], colg[:, 0:1], None, ALU.mult)
            i = vector.tensor_scalar(xqb[:, c, :], xq[:, c, :], -128.0 + 1e-5, 128.0 - 1e-5,
                                     ALU.max, ALU.min)
            P.inc(i, "vv")
    P.block.vector(v_xq)
    q_done = P.snap()

    if dbg:
        sc_d = P.dout("sc_dbg", [1, 32])
        colg_d = P.dout("colg_dbg", [128, 4])
        wq_d2 = P.dout("wq_dbg", [D, D], BF16)
        xqb_d = P.dout("xqb_dbg", [D, T], BF16)
        wdb = P.snap()

        def st_dbg(sync, w=wdb):
            P.wait(sync, w)
            i = sync.dma_start(out=sc_d[:, :], in_=sc[:, :])
            P.inc(i, "st", 16)
            i = sync.dma_start(out=colg_d[:, :], in_=colg[:, :])
            P.inc(i, "st", 16)
            i = sync.dma_start(out=wq_d2.rearrange("(c p) o -> p c o", p=128), in_=wq16[:, :, :])
            P.inc(i, "st", 16)
            i = sync.dma_start(out=xqb_d.rearrange("(c p) t -> p c t", p=128), in_=xqb[:, :, :])
            P.inc(i, "st", 16)
        P.block.sync(st_dbg)

    for oc in range(NCH):
        for s in range(NSL):
            sb = slice(512 * s, 512 * (s + 1))
            n = oc * NSL + s
            vvp = P.cnt["vv"]

            def ld_x(gp_, oc=oc, sb=sb, vvp=vvp):
                P.wait(gp_, {"vv": vvp})
                i = gp_.dma_start(out=xt[:, :], in_=x2_d.rearrange("(c p) t -> p c t", p=128)[:, oc, sb])
                P.inc(i, "ld2", 16)
                i = gp_.dma_start(out=s0t[:, :], in_=s2_d.rearrange("(c p) t -> p c t", p=128)[:, oc, sb])
                P.inc(i, "ld2", 16)
            P.block.gpsimd(ld_x)
            ld_n = P.cnt["ld2"]
            ssp = P.cnt["ss"]

            def t_y(tensor, oc=oc, sb=sb, n=n, q_done=q_done, ssp=ssp):
                P.wait(tensor, {"vv": q_done["vv"]})
                tensor.wait_ge(P.sems["ss"], ssp)
                for c in range(NCH):
                    i = tensor.matmul(ps[n % 2][:, :], wq16[:, c, 128 * oc:128 * (oc + 1)],
                                      xqb[:, c, sb], start=(c == 0), stop=(c == NCH - 1))
                P.inc(i, "mm")
            P.block.tensor(t_y)
            mm_y = P.cnt["mm"]

            def s_y(scalar, n=n, mm_y=mm_y):
                scalar.wait_ge(P.sems["mm"], mm_y)
                scalar.wait_ge(P.sems["vv"], P.cnt["vv"])
                i = scalar.activation(ev[:, :], ps[n % 2][:, :], AF.Silu, scale=colg[:, 1:2])
                P.inc(i, "ss")
            P.block.scalar(s_y)
            ss_y = P.cnt["ss"]
            stp = P.cnt["st"]

            def v_y(vector, ss_y=ss_y, stp=stp, ld_n=ld_n):
                vector.wait_ge(P.sems["ss"], ss_y)
                vector.wait_ge(P.sems["ld2"], ld_n)
                vector.wait_ge(P.sems["st"], stp)
                vector.tensor_tensor(ev[:, :], ev[:, :], s0t[:, :], ALU.mult)
                i = vector.tensor_tensor(xt[:, :], xt[:, :], ev[:, :], ALU.add)
                P.inc(i, "vv")
            P.block.vector(v_y)
            vvn = P.cnt["vv"]

            def st_y(sync, oc=oc, sb=sb, vvn=vvn):
                sync.wait_ge(P.sems["vv"], vvn)
                i = sync.dma_start(out=x3_d.rearrange("(c p) t -> p c t", p=128)[:, oc, sb],
                                   in_=xt[:, :])
                P.inc(i, "st", 16)
            P.block.sync(st_y)

    def fin(sync):
        sync.wait_ge(P.sems["st"], P.cnt["st"])
    P.block.sync(fin)
    return P.snap()


def emit_l3_proj(P, cm):
    xnb_d = P.din("xn3b", [D, T], BF16)
    wq_d = P.din("l3_wq", [D, D])
    bq_d = P.din("l3_bq", [128, NCH])
    wl_d = P.din("l3_wl", [D, 256])
    bl_d = P.din("l3_bl", [128, 2])
    wk_d = P.din("l3_wk", [256, D])
    bk_d = P.din("l3_bk", [128, NCH])
    wv_d = P.din("l3_wv", [256, D])
    bv_d = P.din("l3_bv", [128, NCH])
    q_d = P.dout("q3", [D, T], BF16)
    k_d = P.dout("k3", [D, T], BF16)
    v_d = P.dout("v3", [T, D], BF16)

    xnb = P.sb("h2_xn", [128, NCH, T], BF16)
    lat = P.sb("h2_lat", [128, 2, T], BF16)
    qf = P.sb("h2_qf", [128, NCH, T], BF16)
    vf = P.sb("h2_vf", [128, NCH, T], BF16)
    vt = [P.sb("h2_vt0", [128, D], BF16), P.sb("h2_vt1", [128, D], BF16)]
    cols = P.sb("h2_cols", [128, 32], F32)
    wcb = P.sb("h2_wcb", [128, NCH, 128], F32)
    wcb16 = P.sb("h2_wcb16", [128, NCH, 128], BF16)
    wcb16b = P.sb("h2_wcb16b", [128, NCH, 128], BF16)
    ps = [P.psum("h2_ps0", [128, 512], F32), P.psum("h2_ps1", [128, 512], F32)]
    psT = [P.psum("h2_psT0", [128, 128], BF16), P.psum("h2_psT1", [128, 128], BF16)]
    bq = cols[:, 0:8]
    bk = cols[:, 8:16]
    bv = cols[:, 16:24]
    bl = cols[:, 24:26]

    def ld0(sync):
        for dst, src in [(bq, bq_d), (bk, bk_d), (bv, bv_d), (bl, bl_d)]:
            i = sync.dma_start(out=dst, in_=src[:])
            P.inc(i, "ld", 16)
    P.block.sync(ld0)

    def ld0g(g):
        i = g.dma_start(out=xnb[:, :, :], in_=xnb_d.rearrange("(c p) t -> p c t", p=128))
        P.inc(i, "ld2", 16)
    P.block.gpsimd(ld0g)
    ld_done = P.snap()

    def mk_evac(dst, bias):
        def evac(n, oc, sb, mmn):
            def s_e(scalar, n=n, oc=oc, sb=sb, mmn=mmn):
                scalar.wait_ge(P.sems["mm"], mmn)
                i = scalar.activation(dst[:, oc, sb], ps[n % 2][:, :], AF.Identity,
                                      bias=bias[:, oc:oc + 1])
                P.inc(i, "ss")
            P.block.scalar(s_e)
        return evac

    wv_q = lambda oc: wq_d.rearrange("(c p) o -> p c o", p=128)[:, :, 128 * oc:128 * (oc + 1)]
    mm_stream(P, cm, wv_q, NCH, xnb, NCH, ps, mk_evac(qf, bq), "h2q", wcb, wcb16,
              pre=ld_done, evac_sem="ss", cast_dma=True, wcb2=wcb, wcb16_2=wcb16b)
    store_fm(P, qf, q_d, P.snap())

    wv_l = lambda oc: wl_d.rearrange("(c p) o -> p c o", p=128)[:, :, 128 * oc:128 * (oc + 1)]
    mm_stream(P, cm, wv_l, 2, xnb, NCH, ps, mk_evac(lat, bl), "h2l", wcb, wcb16,
              pre=P.snap(), evac_sem="ss", cast_dma=True, wcb2=wcb, wcb16_2=wcb16b)
    lat_done = P.snap()

    wv_k = lambda oc: wk_d.rearrange("(c p) o -> p c o", p=128)[:, :, 128 * oc:128 * (oc + 1)]
    mm_stream(P, cm, wv_k, NCH, lat, 2, ps, mk_evac(qf, bk), "h2k",
              wcb[:, 0:2, :], wcb16[:, 0:2, :], pre=lat_done, evac_sem="ss",
              x_of=lambda ic, sl: lat[:, ic, sl], cast_dma=True,
              wcb2=wcb[:, 0:2, :], wcb16_2=wcb16b[:, 0:2, :])
    store_fm(P, qf, k_d, P.snap())

    wv_v = lambda oc: wv_d.rearrange("(c p) o -> p c o", p=128)[:, :, 128 * oc:128 * (oc + 1)]
    mm_stream(P, cm, wv_v, NCH, lat, 2, ps, mk_evac(vf, bv), "h2v",
              wcb[:, 0:2, :], wcb16[:, 0:2, :], pre=P.snap(), evac_sem="ss",
              x_of=lambda ic, sl: lat[:, ic, sl], cast_dma=True,
              wcb2=wcb[:, 0:2, :], wcb16_2=wcb16b[:, 0:2, :])
    v_done = P.snap()

    for tt in range(16):
        mm0 = P.cnt["mm"]
        vv0 = P.cnt["vv"]
        stp = P.cnt["st"]

        def t_b(tensor, tt=tt, mm0=mm0, vv0=vv0, v_done=v_done):
            P.wait(tensor, {"ss": v_done["ss"]})
            for c in range(NCH):
                if c >= 2:
                    tensor.wait_ge(P.sems["vv"], vv0 + c - 1)
                i = tensor.transpose(psT[c % 2][:, :], vf[:, c, 128 * tt:128 * (tt + 1)],
                                     cm.identb[:])
                P.inc(i, "mm")
        P.block.tensor(t_b)

        def v_b(vector, tt=tt, mm0=mm0, stp=stp):
            if tt >= 2:
                vector.wait_ge(P.sems["st"], stp - 16)
            for c in range(NCH):
                vector.wait_ge(P.sems["mm"], mm0 + c + 1)
                i = vector.tensor_copy(vt[tt % 2][:, 128 * c:128 * (c + 1)], psT[c % 2][:, :])
                P.inc(i, "vv")
        P.block.vector(v_b)
        vvn = P.cnt["vv"]

        def st_b(sync, tt=tt, vvn=vvn):
            sync.wait_ge(P.sems["vv"], vvn)
            i = sync.dma_start(out=v_d[128 * tt:128 * (tt + 1), :], in_=vt[tt % 2][:, :])
            P.inc(i, "st", 16)
        P.block.sync(st_b)

    def fin(sync):
        sync.wait_ge(P.sems["st"], P.cnt["st"])
    P.block.sync(fin)
    return P.snap()


def emit_l3_attn(P, cm, dbg=False):
    q_d = P.din("q3", [D, T], BF16)
    kg_d = P.din("k3_all", [D, TT], BF16)
    v_d = P.din("v3_all", [TT, D], BF16)
    wo_d = P.din("l3_wo", [D, D])
    bo_d = P.din("l3_bo", [128, NCH])
    s3_d = P.dout("s3", [D, T])

    qg = P.sb("h3_q", [128, NCH, T], BF16)
    kga = P.sb("h3_k", [128, NCH, TT], BF16)
    vaug = P.sb("h3_vaug", [128, 32, 65], BF16)
    pt = [P.sb("h3_pt0", [128, 512], BF16), P.sb("h3_pt1", [128, 512], BF16)]
    av = P.sb("h3_av", [128, NCH, T], BF16)
    avt = P.sb("h3_avt", [64, 512], F32)
    lr = P.sb("h3_lr", [1, 512], F32)
    ev = P.sb("h3_ev", [128, 512], F32)
    bo = P.sb("h3_bo", [128, NCH], F32)
    wcb = P.sb("h3_wcb", [128, NCH, 128], F32)
    wcb16 = P.sb("h3_wcb16", [128, NCH, 128], BF16)
    wcb16b = P.sb("h3_wcb16b", [128, NCH, 128], BF16)
    ps_s = [P.psum("h3_ps0", [128, 512], F32), P.psum("h3_ps1", [128, 512], F32)]
    ps_av = P.psum("h3_psav", [65, 512], F32)
    ps_b = P.psum("h3_psb", [64, 512], F32)

    def ld0(sync):
        i = sync.dma_start(out=bo[:, :], in_=bo_d[:, :])
        P.inc(i, "ld", 16)
        i = sync.dma_start(out=qg[:, :, :], in_=q_d.rearrange("(c p) t -> p c t", p=128))
        P.inc(i, "ld", 16)
        i = sync.dma_start(out=kga[:, :, :], in_=kg_d.rearrange("(c p) t -> p c t", p=128))
        P.inc(i, "ld", 16)
    P.block.sync(ld0)
    ld_k = P.snap()

    def g_init(g):
        i = g.memset(vaug[:, :, 64:65], 1.0)
        P.inc(i, "gp")
    P.block.gpsimd(g_init)
    init_done = P.snap()

    for h in range(16):
        ch, hb = h // 2, 64 * (h % 2)
        prev_h = P.snap()

        def ld_v(sync, h=h, prev_h=prev_h):
            P.wait(sync, {"mm2": prev_h["mm2"], "gp": init_done["gp"]})
            i = sync.dma_start(out=vaug[:, :, 0:64],
                               in_=v_d.rearrange("(k p) d -> p k d", p=128)[:, :, 64 * h:64 * h + 64])
            P.inc(i, "ld", 16)
        P.block.sync(ld_v)
        ld_vh = P.snap()

        for j in range(NSL):
            sl = slice(512 * j, 512 * (j + 1))
            N = 32
            mm0, ss0, mm20 = P.cnt["mm"], P.cnt["ss"], P.cnt["mm2"]

            def t_body(tensor, h=h, ch=ch, hb=hb, sl=sl, mm0=mm0, ss0=ss0, mm20=mm20,
                       vve=P.cnt["vv"]):
                P.wait(tensor, ld_k)
                P.wait(tensor, ld_vh)
                for n in range(N):
                    if n >= 2:
                        tensor.wait_ge(P.sems["ss"], ss0 + n - 1)
                    i = tensor.matmul(ps_s[n % 2][:, :],
                                      kga[hb:hb + 64, ch, 128 * n:128 * (n + 1)],
                                      qg[hb:hb + 64, ch, sl], start=True, stop=True)
                    P.inc(i, "mm")
                    if n >= 1:
                        tensor.wait_ge(P.sems["ss"], ss0 + n)
                        if n == 1:
                            tensor.wait_ge(P.sems["vv"], vve)
                        i = tensor.matmul(ps_av[:, :], vaug[:, n - 1, :], pt[(n - 1) % 2][:, :],
                                          start=(n - 1 == 0), stop=False)
                        P.inc(i, "mm2")
                tensor.wait_ge(P.sems["ss"], ss0 + N)
                i = tensor.matmul(ps_av[:, :], vaug[:, N - 1, :], pt[(N - 1) % 2][:, :],
                                  start=False, stop=True)
                P.inc(i, "mm2")
            P.block.tensor(t_body)

            def s_body(scalar, mm0=mm0, mm20=mm20):
                for n in range(N):
                    if n >= 2:
                        scalar.wait_ge(P.sems["mm2"], mm20 + n - 1)
                    scalar.wait_ge(P.sems["mm"], mm0 + n + 1)
                    i = scalar.activation(pt[n % 2][:, :], ps_s[n % 2][:, :], AF.Exp,
                                          scale=0.125)
                    P.inc(i, "ss")
            P.block.scalar(s_body)
            mm2_av = P.cnt["mm2"]

            def v_evac(vector, mm2_av=mm2_av):
                vector.wait_ge(P.sems["mm2"], mm2_av)
                vector.tensor_copy(avt[:, :], ps_av[0:64, :])
                vector.tensor_copy(lr[:, :], ps_av[64:65, :])
                safe_recip(P, vector, lr[:, :], cm.spc)
                i = vector.engine_nop()
                P.inc(i, "vv")
            P.block.vector(v_evac)
            vv_ev = P.cnt["vv"]

            def t_bc(tensor, vv_ev=vv_ev):
                tensor.wait_ge(P.sems["vv"], vv_ev)
                i = tensor.matmul(ps_b[:, :], cm.ones_rowf[:, 0:64], lr[:, :],
                                  start=True, stop=True)
                P.inc(i, "mm")
            P.block.tensor(t_bc)
            mm_bc = P.cnt["mm"]

            def v_nrm(vector, ch=ch, hb=hb, sl=sl, mm_bc=mm_bc):
                vector.wait_ge(P.sems["mm"], mm_bc)
                i = vector.tensor_tensor(av[hb:hb + 64, ch, sl], avt[:, :], ps_b[:, :], ALU.mult)
                P.inc(i, "vv")
            P.block.vector(v_nrm)
    av_done = P.snap()
    if dbg:
        av_d = P.dout("av_dbg", [D, T], BF16)
        store_fm(P, av, av_d, av_done)
        av_done = P.snap()

    ps = ps_s

    def evac_o(n, oc, sb, mmn):
        stp = P.cnt["st"]

        def s_e(scalar, n=n, oc=oc, sb=sb, mmn=mmn, stp=stp):
            scalar.wait_ge(P.sems["mm"], mmn)
            scalar.wait_ge(P.sems["st"], stp)
            i = scalar.activation(ev[:, :], ps[n % 2][:, :], AF.Identity, bias=bo[:, oc:oc + 1])
            P.inc(i, "ss")
        P.block.scalar(s_e)
        ssn = P.cnt["ss"]

        def st_e(sync, oc=oc, sb=sb, ssn=ssn):
            sync.wait_ge(P.sems["ss"], ssn)
            i = sync.dma_start(out=s3_d.rearrange("(c p) t -> p c t", p=128)[:, oc, sb],
                               in_=ev[:, :])
            P.inc(i, "st", 16)
        P.block.sync(st_e)

    wv_o = lambda oc: wo_d.rearrange("(c p) o -> p c o", p=128)[:, :, 128 * oc:128 * (oc + 1)]
    mm_stream(P, cm, wv_o, NCH, av, NCH, ps, evac_o, "h3o", wcb, wcb16,
              pre=av_done, evac_sem="ss", cast_dma=True, wcb2=wcb, wcb16_2=wcb16b)

    def fin(sync):
        sync.wait_ge(P.sems["st"], P.cnt["st"])
    P.block.sync(fin)
    return P.snap()


def e2m3_round(P, cm, vector, x, sg, a, u, t2, scr1):
    MAGIC = 12582912.0
    vector.tensor_single_scalar(u[:, :], a[:, :], 2.0, ALU.is_ge)
    vector.tensor_single_scalar(t2[:, :], a[:, :], 4.0, ALU.is_ge)
    vector.tensor_tensor(u[:, :], u[:, :], t2[:, :], ALU.add)
    vgap(cm, vector)
    vector.tensor_tensor(t2[:, :], u[:, :], u[:, :], ALU.mult)
    vgap(cm, vector)
    vector.tensor_scalar(scr1[:, :], u[:, :], -5.0, 8.0, ALU.mult, ALU.add)
    vector.tensor_tensor(scr1[:, :], scr1[:, :], t2[:, :], ALU.add)
    vector.tensor_scalar(u[:, :], u[:, :], 0.0625, None, ALU.mult)
    vector.tensor_scalar(t2[:, :], t2[:, :], 0.0625, 0.125, ALU.mult, ALU.add)
    vector.tensor_tensor(u[:, :], u[:, :], t2[:, :], ALU.add)
    vgap(cm, vector)
    vector.tensor_tensor(a[:, :], a[:, :], scr1[:, :], ALU.mult)
    vector.tensor_scalar(a[:, :], a[:, :], MAGIC, -MAGIC, ALU.add, ALU.add)
    vgap(cm, vector)
    vector.tensor_tensor(a[:, :], a[:, :], u[:, :], ALU.mult)


def emit_l3_ffn(P, cm, dbg=False):
    xtm_d = P.din("xn3tm", [T, D], BF16)
    w_d = P.din("l3f_w", [D, D])
    b_d = P.din("l3f_b", [128, NCH])
    x3_d = P.din("x3", [D, T])
    s3_d = P.din("s3", [D, T])
    out_d = P.dout("final", [D, T])

    wf = P.sb("h4_wf", [128, NCH, D], F32)
    wdb = P.sb("h4_wdb", [128, NCH, D], BF16)
    wdT = P.sb("h4_wdT", [128, NCH, D], BF16)
    xt = P.sb("h4_xt", [128, D], F32)
    xfm = P.sb("h4_xfm", [128, NCH, T], BF16)
    sg = P.sb("h4_sg", [128, D], F32)
    a = P.sb("h4_a", [128, D], F32)
    u = P.sb("h4_u", [128, D], F32)
    t2 = P.sb("h4_t2", [128, D], F32)
    s1 = P.sb("h4_s1", [128, D], F32)
    scl = P.sb("h4_scl", [128, 32], F32)
    inv = P.sb("h4_inv", [128, 32], F32)
    bias = P.sb("h4_bias", [128, NCH], F32)
    ev = P.sb("h4_ev", [128, 512], F32)
    rx = P.sb("h4_rx", [128, 512], F32)
    rs = P.sb("h4_rs", [128, 512], F32)
    ps = [P.psum("h4_ps0", [128, 512], F32), P.psum("h4_ps1", [128, 512], F32)]
    psT = [P.psum("h4_psT0", [128, 128], BF16), P.psum("h4_psT1", [128, 128], BF16)]

    def ld0(sync):
        i = sync.dma_start(out=wf[:, :, :], in_=w_d.rearrange("(c p) o -> p c o", p=128))
        P.inc(i, "ld", 16)
        i = sync.dma_start(out=bias[:, :], in_=b_d[:, :])
        P.inc(i, "ld", 16)
    P.block.sync(ld0)
    ld_done = P.snap()

    def quant_tile(src_view, dst_view, pre, tag):
        vv0 = P.cnt["vv"]

        def s_sgn(scalar, pre=pre):
            P.wait(scalar, pre)
            i = scalar.activation(sg[:, :], src_view, AF.Sign)
            P.inc(i, "ss")
        P.block.scalar(s_sgn)
        ss_s = P.cnt["ss"]

        def v_q(vector, pre=pre, ss_s=ss_s):
            vector.wait_ge(P.sems["ss"], ss_s)
            P.wait(vector, pre)
            vector.tensor_tensor(a[:, :], src_view, sg[:, :], ALU.mult)
            av3 = a[:, :].rearrange("p (b k) -> p b k", b=32)
            vector.tensor_reduce(scl[:, :].rearrange("p (b k) -> p b k", k=1), av3, AX.X, ALU.max)
            vgap(cm, vector)
            vector.tensor_scalar(scl[:, :], scl[:, :], 1.0 / 7.5, 1e-30, ALU.mult, ALU.max)
            vgap(cm, vector)
            vgap(cm, vector)
            vector.tensor_copy(inv[:, :], scl[:, :])
            vgap(cm, vector)
            vector.reciprocal(inv[:, :], inv[:, :])
            vgap(cm, vector)
            vgap(cm, vector)
            invb = inv[:, :].rearrange("p (b k) -> p b k", k=1).to_broadcast([128, 32, 32])
            vector.tensor_tensor(a[:, :].rearrange("p (b k) -> p b k", b=32), 
                                 a[:, :].rearrange("p (b k) -> p b k", b=32), invb, ALU.mult)
            e2m3_round(P, cm, vector, None, sg, a, u, t2, s1)
            sclb = scl[:, :].rearrange("p (b k) -> p b k", k=1).to_broadcast([128, 32, 32])
            vector.tensor_tensor(a[:, :].rearrange("p (b k) -> p b k", b=32),
                                 a[:, :].rearrange("p (b k) -> p b k", b=32), sclb, ALU.mult)
            i = vector.tensor_tensor(dst_view, a[:, :], sg[:, :], ALU.mult)
            P.inc(i, "vv")
        P.block.vector(v_q)
        return P.snap()

    qw_done = ld_done
    for c in range(NCH):
        qw_done = quant_tile(wf[:, c, :], wdb[:, c, :], qw_done, f"w{c}")

    if dbg:
        wd_d2 = P.dout("wd_dbg", [D, D], BF16)
        store_fm(P, wdb, wd_d2, qw_done)
        qw_done = P.snap()

    for tt in range(NCH):
        mm0 = P.cnt["mm"]
        vv0 = P.cnt["vv"]

        def t_b(tensor, tt=tt, mm0=mm0, vv0=vv0, qw_done=qw_done):
            P.wait(tensor, {"vv": qw_done["vv"]})
            for c in range(NCH):
                if c >= 2:
                    tensor.wait_ge(P.sems["vv"], vv0 + c - 1)
                i = tensor.transpose(psT[c % 2][:, :], wdb[:, c, 128 * tt:128 * (tt + 1)],
                                     cm.identb[:])
                P.inc(i, "mm")
        P.block.tensor(t_b)

        def v_b(vector, tt=tt, mm0=mm0):
            for c in range(NCH):
                vector.wait_ge(P.sems["mm"], mm0 + c + 1)
                i = vector.tensor_copy(wdT[:, tt, 128 * c:128 * (c + 1)], psT[c % 2][:, :])
                P.inc(i, "vv")
        P.block.vector(v_b)
    wT_done = P.snap()

    xdb_t = P.sb("h4_xdb", [128, D], BF16)
    for tt in range(16):
        prev = P.snap()

        def ld_x(g, tt=tt, prev=prev):
            P.wait(g, {"vv": prev["vv"], "mm": prev["mm"]})
            i = g.dma_start(out=xt[:, :], in_=xtm_d[128 * tt:128 * (tt + 1), :])
            P.inc(i, "ld2", 16)
        P.block.gpsimd(ld_x)
        q_done = quant_tile(xt[:, :], xdb_t[:, :], P.snap(), f"x{tt}")

        mm0 = P.cnt["mm"]
        vv0 = P.cnt["vv"]

        def t_b(tensor, tt=tt, mm0=mm0, vv0=vv0, q_done=q_done):
            P.wait(tensor, {"vv": q_done["vv"]})
            for c in range(NCH):
                if c >= 2:
                    tensor.wait_ge(P.sems["vv"], vv0 + c - 1)
                i = tensor.transpose(psT[c % 2][:, :], xdb_t[:, 128 * c:128 * (c + 1)],
                                     cm.identb[:])
                P.inc(i, "mm")
        P.block.tensor(t_b)

        def v_b(vector, tt=tt, mm0=mm0):
            for c in range(NCH):
                vector.wait_ge(P.sems["mm"], mm0 + c + 1)
                i = vector.tensor_copy(xfm[:, c, 128 * tt:128 * (tt + 1)], psT[c % 2][:, :])
                P.inc(i, "vv")
        P.block.vector(v_b)
    qx_done = P.snap()

    for oc in range(NCH):
        for s in range(NSL):
            sb = slice(512 * s, 512 * (s + 1))
            n = oc * NSL + s
            vvp = P.cnt["vv"]

            def ld_rs(g, oc=oc, sb=sb, vvp=vvp):
                P.wait(g, {"vv": vvp})
                i = g.dma_start(out=rx[:, :], in_=x3_d.rearrange("(c p) t -> p c t", p=128)[:, oc, sb])
                P.inc(i, "ld2", 16)
                i = g.dma_start(out=rs[:, :], in_=s3_d.rearrange("(c p) t -> p c t", p=128)[:, oc, sb])
                P.inc(i, "ld2", 16)
            P.block.gpsimd(ld_rs)
            ld_n = P.cnt["ld2"]
            ssp = P.cnt["ss"]

            def t_y(tensor, oc=oc, sb=sb, n=n, qx_done=qx_done, ssp=ssp, wT_done=wT_done):
                P.wait(tensor, {"vv": max(qx_done["vv"], wT_done["vv"])})
                tensor.wait_ge(P.sems["ss"], ssp)
                for c in range(NCH):
                    i = tensor.matmul(ps[n % 2][:, :], wdT[:, c, 128 * oc:128 * (oc + 1)],
                                      xfm[:, c, sb], start=(c == 0), stop=(c == NCH - 1))
                P.inc(i, "mm")
            P.block.tensor(t_y)
            mm_y = P.cnt["mm"]

            def s_y(scalar, n=n, oc=oc, mm_y=mm_y):
                scalar.wait_ge(P.sems["mm"], mm_y)
                scalar.wait_ge(P.sems["vv"], P.cnt["vv"])
                i = scalar.activation(ev[:, :], ps[n % 2][:, :], AF.Silu, bias=bias[:, oc:oc + 1])
                P.inc(i, "ss")
            P.block.scalar(s_y)
            ss_y = P.cnt["ss"]
            stp = P.cnt["st"]

            def v_y(vector, ss_y=ss_y, stp=stp, ld_n=ld_n):
                vector.wait_ge(P.sems["ss"], ss_y)
                vector.wait_ge(P.sems["ld2"], ld_n)
                vector.wait_ge(P.sems["st"], stp)
                vector.tensor_tensor(ev[:, :], ev[:, :], rs[:, :], ALU.mult)
                i = vector.tensor_tensor(rx[:, :], rx[:, :], ev[:, :], ALU.add)
                P.inc(i, "vv")
            P.block.vector(v_y)
            vvn = P.cnt["vv"]

            def st_y(sync, oc=oc, sb=sb, vvn=vvn):
                sync.wait_ge(P.sems["vv"], vvn)
                i = sync.dma_start(out=out_d.rearrange("(c p) t -> p c t", p=128)[:, oc, sb],
                                   in_=rx[:, :])
                P.inc(i, "st", 16)
            P.block.sync(st_y)

    def fin(sync):
        sync.wait_ge(P.sems["st"], P.cnt["st"])
    P.block.sync(fin)
    return P.snap()



def _const_clock_sel():
    cs = np.zeros((16, NCH, 128), np.float32)
    for c in range(NCH):
        for i in range(128):
            cs[2 * c + i // 64, c, i] = 1.0
    return cs


def _const_wedge():
    w = np.full((4, 128, 512), -1e30, np.float32)
    pp, ff = np.meshgrid(np.arange(128), np.arange(512), indexing="ij")
    for i in range(4):
        w[i][ff - pp - 128 * i >= 0] = 0.0
    return w


def _const_bkt():
    NUM_BUCKETS, MAX_DIST = 32, 128
    max_exact = NUM_BUCKETS // 2
    ln_ratio = np.log(MAX_DIST / max_exact)
    jj = (np.arange(128)[:, None, None] + 128 * np.arange(2)[None, :, None])
    ii = np.arange(256)[None, None, :]
    n = np.maximum(ii - jj, 0)
    is_small = n < max_exact
    val_large = max_exact + (np.log(np.maximum(n, 1).astype(np.float32) / max_exact) / ln_ratio * (NUM_BUCKETS - max_exact)).astype(np.int32)
    val_large = np.minimum(val_large, NUM_BUCKETS - 1)
    bucket = np.where(is_small, n, val_large)
    return np.ascontiguousarray(bucket.astype(np.float32)).reshape(128, 2, 256)


def _const_lw(h):
    gi = np.arange(TT, dtype=np.float32)[:, None]
    gj = (np.arange(T, dtype=np.float32) + np.float32(2048.0 * h))[None, :]
    loc = np.abs(gi - gj).astype(np.float32)
    return np.ascontiguousarray((np.float32(1.0) / (loc + np.float32(1e-7))) * (loc != 0))


_PROG_CACHE = {}


def _prog(name, emitter):
    if name not in _PROG_CACHE:
        with Prog(name) as P:
            cm = Common(P)
            emitter(P, cm)
        _PROG_CACHE[name] = P.finish()
    return _PROG_CACHE[name]


def _launch(name, emitter, ims):
    nc = _prog(name, emitter)
    res = run_bass_kernel_spmd(nc, ims, core_ids=list(range(8)))
    return res.results


def kernel(**I):
    import ml_dtypes
    BF = ml_dtypes.bfloat16
    NC = 8
    bb = [dict() for _ in range(NC)]

    x_full = np.asarray(I["L0_attn_x"])
    for c in range(NC):
        b, h = c // 2, c % 2
        bb[c]["x_fm"] = _fmT(x_full[b, 2048 * h:2048 * (h + 1)])

    zeros_pc = np.zeros((128, NCH), np.float32)
    csel = _const_clock_sel()
    wedge = _const_wedge()
    bkt = _const_bkt()
    lw = [_const_lw(0), _const_lw(1)]
    rope = [rope_tables(0), rope_tables(1)]
    tr = lambda a: np.ascontiguousarray(np.asarray(a).T)

    r = _launch("a1", lambda P, cm: emit_ln0(P, cm, lng="ln1_g", lnb="ln1_b", ln1_out="ln1"),
                [dict(x_fm=bb[c]["x_fm"], ln1_g=_pc(I["L0_attn_ln1_g"]),
                      ln1_b=_pc(I["L0_attn_ln1_b"])) for c in range(NC)])
    for c in range(NC):
        bb[c].update(r[c])

    r = _launch("a2", emit_projA,
                [dict(ln1=bb[c]["ln1"], rope_cos=rope[c % 2][0], rope_sin=rope[c % 2][1],
                      l0_wq=I["L0_attn_Wq"], l0_wk=I["L0_attn_Wk"], l0_wv=I["L0_attn_Wv"])
                 for c in range(NC)])
    for c in range(NC):
        bb[c].update(r[c])

    r = _launch("a3", emit_projB,
                [dict(ln1=bb[c]["ln1"], l0_gate_w=I["L0_attn_gate_w"],
                      l0_gate_b=_pc(I["L0_attn_gate_b"]), l0_p_w=I["L0_attn_p_w"],
                      l0_p_b=_pc(I["L0_attn_p_b"]), l0_clock_w=I["L0_attn_clock_w"],
                      l0_clock_b=np.ascontiguousarray(np.asarray(I["L0_attn_clock_b"]).reshape(16, 1)))
                 for c in range(NC)])
    for c in range(NC):
        bb[c].update(r[c])

    r = _launch("b1", emit_gate,
                [dict(gj=bb[c]["gj"], p=bb[c]["p"], clock=bb[c]["clock"], k0=bb[c]["k0"],
                      gj_prev=bb[c ^ 1]["gjs"] if c % 2 == 1 else zeros_pc,
                      pmx_other=bb[c ^ 1]["pmx"], pmx=bb[c]["pmx"], clock_sel=csel)
                 for c in range(NC)])
    for c in range(NC):
        bb[c].update(r[c])

    ims = []
    for c in range(NC):
        h = c % 2
        vis = np.zeros((128, 2), np.float32)
        vis[:, 0] = 0.125 if h == 1 else 0.0
        vis[:, 1] = 0.0 if h == 1 else -88.0
        ims.append(dict(q0=bb[c]["q0"], gcp=bb[c]["gcp"], p_exp=bb[c]["p_exp"],
                        pexp_prev=bb[c ^ 1]["pexp_sum"] if h == 1 else zeros_pc,
                        kg_all=np.concatenate([bb[c]["kg"], bb[c ^ 1]["kg"]], axis=1),
                        v_all=np.concatenate([bb[c]["v0"], bb[c ^ 1]["v0"]], axis=0),
                        vis=vis, wedge=wedge))
    r = _launch("c", emit_attn0, ims)
    for c in range(NC):
        bb[c].update(r[c])

    r = _launch("d", emit_mlp0,
                [dict(av=bb[c]["av"], xn0=bb[c]["xn0"], l0_out_w=I["L0_attn_out_w"],
                      l0_out_b=_pc(I["L0_attn_out_b"]), l0_ln2_g=_pc(I["L0_attn_ln2_g"]),
                      l0_ln2_b=_pc(I["L0_attn_ln2_b"]), l0_mlp_w1=I["L0_attn_mlp_w1"],
                      l0_mlp_b1=_pc(I["L0_attn_mlp_b1"]), l0_mlp_w2=I["L0_attn_mlp_w2"],
                      l0_mlp_b2=_pc(I["L0_attn_mlp_b2"])) for c in range(NC)])
    for c in range(NC):
        bb[c].update(r[c])

    moe_b1 = np.ascontiguousarray(np.asarray(I["L0_ffn_b1"]).reshape(16, 4, 128).transpose(0, 2, 1))
    moe_b2 = np.ascontiguousarray(np.asarray(I["L0_ffn_b2"]).reshape(16, 128, 1))
    r = _launch("e", emit_moe,
                [dict(xn0=bb[c]["xn0"], moe_wm=I["L0_ffn_Wm"], moe_bm=_pc(I["L0_ffn_bm"]),
                      moe_emb=I["L0_ffn_expert_emb"], moe_w1=I["L0_ffn_W1"], moe_b1=moe_b1,
                      moe_w2=I["L0_ffn_W2"], moe_b2=moe_b2, moe_wo=I["L0_ffn_Wo"],
                      moe_bo=_pc(I["L0_ffn_bo"]), x_fm=bb[c]["x_fm"], s0=bb[c]["s0"])
                 for c in range(NC)])
    for c in range(NC):
        bb[c].update(r[c])

    r = _launch("f1", lambda P, cm: emit_lnx(P, cm, "x1", "f1", out_bf="xn1b", out_tm="xn1tm"),
                [dict(x1=bb[c]["x1"]) for c in range(NC)])
    for c in range(NC):
        bb[c].update(r[c])

    ims = []
    for c in range(NC):
        lo, hi = (c & ~1), (c | 1)
        xall = np.concatenate([bb[lo]["xn1b"], bb[hi]["xn1b"]], axis=1)
        xall_tm = np.concatenate([bb[lo]["xn1tm"], bb[hi]["xn1tm"]], axis=0)
        ims.append(dict(xn1b=bb[c]["xn1b"], xn1_all=xall, xn1_all_tm=xall_tm,
                        l1_w=I["L1_attn_W"], l1_b=_pc(I["L1_attn_b"]), l1_lw=lw[c % 2]))
    r = _launch("f2", emit_l1_attn, ims)
    for c in range(NC):
        bb[c].update(r[c])

    l1w = dict(l1_w1t=tr(I["L1_ffn_w1_w"]), l1_w2t=tr(I["L1_ffn_w2_w"]),
               l1_h1t=tr(I["L1_ffn_h1_w"]), l1_h2t=tr(I["L1_ffn_h2_w"]),
               l1_w3t=tr(I["L1_ffn_w3_w"]), l1_s1=_pc(I["L1_ffn_w1_s"]),
               l1_s2=_pc(I["L1_ffn_w2_s"]), l1_sh1=_pc(I["L1_ffn_h1_s"]),
               l1_sh2=_pc(I["L1_ffn_h2_s"]), l1_s3=_pc(I["L1_ffn_w3_s"]))
    r = _launch("f3", emit_l1_ffn,
                [dict(xn1b=bb[c]["xn1b"], x1=bb[c]["x1"], s1=bb[c]["s1"], **l1w)
                 for c in range(NC)])
    for c in range(NC):
        bb[c].update(r[c])

    r = _launch("g1", emit_ln2x,
                [dict(x2=bb[c]["x2"], l2f_g=_pc(I["L2_ffn_ln_gamma"]),
                      l2f_b=_pc(I["L2_ffn_ln_beta"])) for c in range(NC)])
    for c in range(NC):
        bb[c].update(r[c])

    r = _launch("g2", emit_l2a,
                [dict(xn2b=bb[c]["xn2b"], l2_lnw=_pc(I["L2_attn_ln_w"]),
                      l2_lnb=_pc(I["L2_attn_ln_b"]), l2_wh=I["L2_attn_Wh"],
                      l2_bh=_pc(I["L2_attn_bh"]),
                      l2_wqk=I["L2_attn_Wqk"],
                      l2_bqk=np.ascontiguousarray(np.asarray(I["L2_attn_bqk"]).reshape(128, 1)),
                      l2_qkw=tr(I["L2_attn_qk_w"]), l2_qkb=tr(I["L2_attn_qk_b"]),
                      l2_tblr=np.ascontiguousarray(np.asarray(I["L2_attn_rel_table"]).reshape(1, 32)),
                      l2_bkt=bkt) for c in range(NC)])
    for c in range(NC):
        bb[c].update(r[c])

    r = _launch("g3", emit_l2b,
                [dict(l2_kvg=bb[c]["l2_kvg"],
                      kv_prev=bb[c ^ 1]["kv_own"] if c % 2 == 1 else np.zeros((128, 2048), np.float32),
                      l2_lq=bb[c]["l2_lq"], l2_gate=bb[c]["l2_gate"], l2_quad=bb[c]["l2_quad"],
                      l2_wout=I["L2_attn_Wout"], l2_bout=_pc(I["L2_attn_bout"]),
                      xn2f=bb[c]["xn2f"]) for c in range(NC)])
    for c in range(NC):
        bb[c].update(r[c])

    gmax_all = np.concatenate([bb[c]["gmax_loc"].reshape(-1) for c in range(NC)]).reshape(1, -1)
    r = _launch("g4", emit_l2ffn,
                [dict(xq_n=bb[c]["xq_n"], gmax_all=gmax_all, l2f_w=I["L2_ffn_weight"],
                      x2=bb[c]["x2"], s2=bb[c]["s2"]) for c in range(NC)])
    for c in range(NC):
        bb[c].update(r[c])

    r = _launch("h1", lambda P, cm: emit_lnx(P, cm, "x3", "h1", out_bf="xn3b", out_tm="xn3tm"),
                [dict(x3=bb[c]["x3"]) for c in range(NC)])
    for c in range(NC):
        bb[c].update(r[c])

    r = _launch("h2a", emit_l3_proj,
                [dict(xn3b=bb[c]["xn3b"], l3_wq=I["L3_attn_Wq"], l3_bq=_pc(I["L3_attn_bq"]),
                      l3_wl=I["L3_attn_Wl"],
                      l3_bl=np.ascontiguousarray(np.asarray(I["L3_attn_bl"]).reshape(2, 128).T),
                      l3_wk=I["L3_attn_Wk"], l3_bk=_pc(I["L3_attn_bk"]),
                      l3_wv=I["L3_attn_Wv"], l3_bv=_pc(I["L3_attn_bv"])) for c in range(NC)])
    for c in range(NC):
        bb[c].update(r[c])

    ims = []
    for c in range(NC):
        lo, hi = (c & ~1), (c | 1)
        ims.append(dict(q3=bb[c]["q3"],
                        k3_all=np.concatenate([bb[lo]["k3"], bb[hi]["k3"]], axis=1),
                        v3_all=np.concatenate([bb[lo]["v3"], bb[hi]["v3"]], axis=0),
                        l3_wo=I["L3_attn_Wo"], l3_bo=_pc(I["L3_attn_bo"])))
    r = _launch("h2b", emit_l3_attn, ims)
    for c in range(NC):
        bb[c].update(r[c])

    r = _launch("h3", emit_l3_ffn,
                [dict(xn3tm=bb[c]["xn3tm"], l3f_w=I["L3_ffn_weight"], l3f_b=_pc(I["L3_ffn_bias"]),
                      x3=bb[c]["x3"], s3=bb[c]["s3"]) for c in range(NC)])

    out = np.zeros((4, 4096, 1024), np.float32)
    for c in range(NC):
        b, h = c // 2, c % 2
        out[b, 2048 * h:2048 * (h + 1), :] = r[c]["final"].T
    return out
